# version 21
# baseline (speedup 1.0000x reference)
import numpy as np
import ml_dtypes
import concourse.bass as bass
import concourse.mybir as mybir
from concourse.bass_utils import run_bass_kernel_spmd
from concourse.bass_core_rust import IndirectOffsetOnAxis
from contextlib import ExitStack

F32 = mybir.dt.float32
BF16 = mybir.dt.bfloat16
U32 = mybir.dt.uint32
I32 = mybir.dt.int32
AF = mybir.ActivationFunctionType
ALU = mybir.AluOpType
AX = mybir.AxisListType.X

NCORE = 8
TOK = 2048
D = 1024
E2 = 2048
SDIM = 128
NEXP = 8
FF = 4096
NT = 16
NG = 4
CAP = 2048
BIGF = float(2 ** 25)
TABN = NEXP * CAP + 16512
EPS = 1e-5

V_TM = 0
QBF = 32768
KBF = 34816
WUV = 36864
XNF = 70656
UGRP = 74752
WOUTC = 36864
SQTC = 53248
UAVC = 61440
USBC = 69632
W1C = 0
W2C = 36864
SFMC = 32768
HBUFC = 69632
AB_COLS = 78848

XFO = 0
SCRO = 2048
PRTO = 3072
BASEO = 5120
LGO = 5632
M1O = 5760
RANKO = 5888
RT = 6016
SUMX, SUMX2, MUO, VVO, RSTDO, NBO = RT, RT + 16, RT + 32, RT + 48, RT + 64, RT + 80
G0O, LOCO, KEPTO, ESO, GFINO = RT + 96, RT + 112, RT + 128, RT + 144, RT + 160
DIDXO, SGMO, T1O, T2O, T3O = RT + 176, RT + 192, RT + 208, RT + 224, RT + 240
INVO, SIDXO, CIDXO, MXO = RT + 256, RT + 272, RT + 288, RT + 304
OFSO = RT + 320
AF_COLS = RT + 336
EXPO = BASEO
RAWVO = BASEO + 128
PFX1O = BASEO + 256
PFX2O = BASEO + 384


class _Fake:
    def __getattr__(self, name):
        return lambda *a, **k: None


class Marks:

    def __init__(self):
        self.record = True
        self.cnt = {}
        self.marks = {}
        self.sems = {}

    def inc(self, inst, sem, n, mark=None):
        if self.record:
            v = self.cnt.get(sem, 0) + n
            self.cnt[sem] = v
            if mark is not None:
                assert mark not in self.marks, mark
                self.marks[mark] = (sem, v)
        if inst is not None:
            inst.then_inc(self.sems[sem], n)

    def wait(self, eng, mark):
        s, v = self.marks[mark] if not self.record else (None, None)
        if self.record:
            assert mark in self.marks, f"forward mark reference: {mark}"
            return
        if not isinstance(eng, _Fake):
            eng.wait_ge(self.sems[s], v)

    def wait_sem(self, eng, sem, v):
        if self.record:
            return
        if not isinstance(eng, _Fake):
            eng.wait_ge(self.sems[sem], v)


def _emit(E, mk, C, debug):
    sy, gp, ac, ve, te = E["sy"], E["gp"], E["ac"], E["ve"], E["te"]
    AB, AFa = C["AB"], C["AFa"]

    def ab(off, n):
        return AB[:, off:off + n]

    def af(off, n):
        return AFa[:, off:off + n]

    PS = C["PS"]
    PT = C["PT"]

    if not isinstance(gp, _Fake):
        rbc = gp.alloc_register("rbc")
        gp.reg_mov(rbc, NEXP * CAP - 1)
        rb2 = gp.alloc_register("rb2")
        gp.reg_mov(rb2, TABN - 1)
    else:
        rbc = NEXP * CAP - 1
        rb2 = TABN - 1

    for name in ["ident", "sut", "onesc", "ones1f", "ones1b", "iota8", "psel",
                 "tokv", "sgbase", "buvfm", "b1fm", "gq", "gk", "bq", "bk"]:
        mk.inc(sy.dma_start(out=C[name + "_sb"][:], in_=C[name + "_d"]), "ds", 16)
    tz = C["tabl"].ap()[0:16384, 0:1].rearrange("(a b) c -> a (b c)", a=128)
    mk.inc(sy.dma_start(out=tz, in_=C["zero_d"]), "ds", 16, "c:tzero")

    wg3 = C["wg"].rearrange("(c p) n -> p c n", p=128)
    wgb3 = C["wgb_sb"][:].rearrange("p (c n) -> p c n", c=8)
    mk.inc(gp.dma_start(out=wgb3, in_=wg3), "dg", 16)
    mk.inc(gp.dma_start(out=C["buvv_sb"][:], in_=C["buv_row"][0:1, 2048:4096]), "dg", 16)
    mk.inc(gp.dma_start(out=C["boutr_sb"][:], in_=C["boutr"]), "dg", 16)
    mk.inc(gp.dma_start(out=C["b2r_sb"][:], in_=C["b2r"]), "dg", 16, "w:small")
    wuv3 = C["wuv"].rearrange("(c p) n -> p c n", p=128)
    wuv_sb3 = ab(WUV, 8 * 4224).rearrange("p (c n) -> p c n", c=8)
    mk.inc(gp.dma_start(out=wuv_sb3[:, :, 2048:4096], in_=wuv3[:, :, 2048:4096]), "dg", 16, "w:v0")
    mk.wait(gp, "w:v0")
    mk.inc(gp.dma_start(out=C["fence_b"].ap()[:, 0:512], in_=ab(WUV, 512)), "dg", 16, "w:v")
    mk.inc(gp.dma_start(out=wuv_sb3[:, :, 0:2048], in_=wuv3[:, :, 0:2048]), "dg", 16)
    mk.inc(gp.dma_start(out=wuv_sb3[:, :, 4096:4224], in_=wuv3[:, :, 4096:4224]), "dg", 16, "w:uv0")
    mk.wait(gp, "w:uv0")
    mk.inc(gp.dma_start(out=C["fence_b"].ap()[:, 0:512], in_=ab(WUV + 4224, 512)), "dg", 16, "w:uv")

    xnb = [C["xnb0"], C["xnb1"]]
    x_ap = C["x"]

    def p1_load(t):
        if t < 2:
            mk.wait(sy, f"pre:{14 + t}")
        if t >= 2:
            mk.wait(sy, f"xn:{t - 2}")
        mk.inc(sy.dma_start(out=af(XFO + (t % 2) * 1024, 1024),
                            in_=x_ap[t * 128:(t + 1) * 128, :]), "ds", 16, f"xf0:{t}")
        mk.wait(sy, f"xf0:{t}")
        mk.inc(sy.dma_start(out=C["fence_f"].ap()[:, 0:128],
                            in_=af(XFO + (t % 2) * 1024 + 896, 128)),
               "ds", 16, f"xf:{t}")

    def p1_pre(t):
        ld = sy if t % 2 == 0 else ac
        sm = "ds" if t % 2 == 0 else "da"
        if t >= 2:
            mk.wait(ld, f"pre:{t - 2}")
        mk.inc(ld.dma_start(out=af(XFO + (t % 2) * 1024, 1024),
                            in_=x_ap[t * 128:(t + 1) * 128, :]), sm, 16, f"pxf0:{t}")
        mk.wait(ld, f"pxf0:{t}")
        mk.inc(ld.dma_start(out=C["fence_f"].ap()[:, 0:128],
                            in_=af(XFO + (t % 2) * 1024 + 896, 128)),
               sm, 16, f"pxf:{t}")
        scr = SCRO if t % 2 == 0 else PRTO
        mk.wait(ac, f"pxf:{t}")
        if t >= 2:
            mk.wait(ac, f"pre:{t - 2}")
        ac.activation(af(scr, 1024), af(XFO + (t % 2) * 1024, 1024), AF.Square)
        mk.inc(ac.drain(), "sa", 1, f"psq:{t}")
        mk.wait(ve, f"pxf:{t}")
        ve.tensor_reduce(af(SUMX + t, 1), af(XFO + (t % 2) * 1024, 1024), AX, ALU.add)
        mk.wait(ve, f"psq:{t}")
        mk.inc(ve.tensor_reduce(af(SUMX2 + t, 1), af(scr, 1024), AX, ALU.add),
               "sv", 1, f"pre:{t}")

    def p1_newton():
        n = 16
        ve.drain()
        ve.tensor_scalar(af(MUO, n), af(SUMX, n), 1.0 / D, None, ALU.mult)
        ve.tensor_scalar(af(VVO, n), af(SUMX2, n), 1.0 / D, EPS, ALU.mult, ALU.add)
        ve.drain()
        ve.tensor_mul(af(T1O, n), af(MUO, n), af(MUO, n))
        ve.drain()
        ve.tensor_sub(af(VVO, n), af(VVO, n), af(T1O, n))
        ve.drain()
        ve.tensor_scalar(af(VVO, n), af(VVO, n), 1e-6, None, ALU.max)
        ve.drain()
        ve.tensor_scalar(af(RSTDO, n), af(VVO, n), -0.5, 1.5, ALU.mult, ALU.add)
        for _ in range(3):
            ve.drain()
            ve.tensor_mul(af(T1O, n), af(RSTDO, n), af(RSTDO, n))
            ve.drain()
            ve.tensor_mul(af(T1O, n), af(VVO, n), af(T1O, n))
            ve.drain()
            ve.tensor_scalar(af(T1O, n), af(T1O, n), -0.5, 1.5, ALU.mult, ALU.add)
            ve.drain()
            ve.tensor_mul(af(RSTDO, n), af(RSTDO, n), af(T1O, n))
        ve.drain()
        ve.tensor_scalar(af(RSTDO, n), af(RSTDO, n), 0.0, 1e4, ALU.max, ALU.min)
        ve.drain()
        ve.tensor_mul(af(T2O, n), af(MUO, n), af(RSTDO, n))
        ve.drain()
        ve.tensor_scalar(af(NBO, n), af(T2O, n), -1.0, None, ALU.mult)
        mk.inc(ve.drain(), "sv", 1, "nball")

    def p1_xn(t):
        mk.wait(ac, "nball")
        mk.wait(ac, f"pxf:{t}") if False else None
        if t >= 1:
            mk.wait(ac, f"trall:{t - 1}")
            mk.wait(ac, f"xnpark:{t - 1}")
        mk.wait(ac, f"xf:{t}")
        mk.inc(ac.activation(xnb[0][:], af(XFO + (t % 2) * 1024, 1024), AF.Identity,
                             bias=af(NBO + t, 1), scale=af(RSTDO + t, 1)),
               "sa", 1, f"xn:{t}")

    def p1_tr(t):
        g, j = t // 4, t % 4
        mk.wait(te, f"xn:{t}")
        for jj in range(8):
            if t * 8 + jj >= 2:
                pj = t * 8 + jj - 2
                mk.wait(te, f"trev:{pj // 8}:{pj % 8}")
            inst = te.transpose(PT[jj % 2][:, 0:128],
                                xnb[0][:, jj * 128:(jj + 1) * 128],
                                C["ident_sb"][:])
            mk.inc(inst, "st", 1, f"tr:{t}:{jj}")
            mk.wait(ve, f"tr:{t}:{jj}")
            if j == 0 and jj == 0 and g > 0:
                mk.wait(ve, f"umm:{g - 1}:15")
            mk.inc(ve.tensor_copy(ab(XNF + jj * 512 + j * 128, 128),
                                  PT[jj % 2][:, 0:128]), "sv", 1, f"trev:{t}:{jj}")
        mk.inc(None, "st", 0)
        if mk.record:
            mk.marks[f"trall:{t}"] = ("st", mk.cnt["st"])

    def p1_body(t):
        g, j = t // 4, t % 4
        mk.wait(te, "w:small")
        mk.wait(te, f"trev:{t}:7")
        if t >= 1:
            mk.wait(te, f"lgev:{t - 1}")
        if j == 0 and t > 0:
            mk.wait(te, f"uev:{t // 4 - 1}:14")
        for f in range(8):
            inst = te.matmul(PS[6][:, 0:8], ab(XNF + f * 512 + j * 128, 128),
                             C["wgb_sb"][:, f * 8:(f + 1) * 8],
                             start=(f == 0), stop=(f == 7))
        mk.inc(inst, "st", 1, f"lgmm:{t}")
        mk.wait(ve, f"lgmm:{t}")
        mk.inc(ve.tensor_copy(af(LGO + t * 8, 8), PS[6][:, 0:8]), "sv", 1, f"lgev:{t}")
        mk.wait(te, "w:v")
        if t >= 1:
            mk.wait(te, f"vev:{t - 1}")
        for eb in range(4):
            te.matmul(PS[eb][:, 0:512], C["ones1b_sb"][:],
                      C["buvv_sb"][0:1, eb * 512:(eb + 1) * 512],
                      start=True, stop=False)
        for f in range(8):
            for eb in range(4):
                inst = te.matmul(PS[eb][:, 0:512],
                                 ab(XNF + f * 512 + j * 128, 128),
                                 ab(WUV + f * 4224 + 2048 + eb * 512, 512),
                                 start=False, stop=(f == 7))
        mk.inc(inst, "st", 1, f"vmm:{t}")
        if t + 1 < NT and (t + 1) % 4 != 0:
            p1_xn(t + 1)
            p1_tr(t + 1)
        mk.wait(ac, f"vmm:{t}")
        for eb in range(4):
            inst = ac.activation(ab(V_TM + t * 2048 + eb * 512, 512),
                                 PS[eb][:, 0:512], AF.Silu)
        mk.inc(inst, "sa", 1, f"vev:{t}")
        if t + 1 < NT and (t + 1) % 4 != 0:
            mk.wait(ac, f"vev:{t}")
            mk.inc(ac.dma_start(out=C["xnp_in"].ap()[(t + 1) * 128:(t + 2) * 128, :],
                                in_=xnb[0][:]), "da", 16, f"xnpark:{t + 1}")

    def p1_tail(g):
        mk.wait(te, f"lgev:{4 * g + 3}")
        for f in range(8):
            inst = te.matmul(PS[6][:, 0:512], ab(WUV + f * 4224 + 4096, 128),
                             ab(XNF + f * 512, 512), start=(f == 0), stop=(f == 7))
        mk.inc(inst, "st", 1, f"bsmm:{g}")
        mk.wait(ac, f"bsmm:{g}")
        if g > 0:
            mk.wait(ac, f"qk:{g - 1}")
        ac.activation(af(BASEO, 512), PS[6][:, 0:512], AF.Silu,
                      bias=C["buvfm_sb"][:, 32:33])
        mk.inc(ac.drain(), "sa", 1, f"bsev:{g}")
        mk.wait(ve, f"bsev:{g}")
        ve.tensor_scalar(ab(QBF + g * 512, 512), af(BASEO, 512),
                         C["gq_sb"][:, 0:1], C["bq_sb"][:, 0:1], ALU.mult, ALU.add)
        ve.tensor_scalar(ab(KBF + g * 512, 512), af(BASEO, 512),
                         C["gk_sb"][:, 0:1], C["bk_sb"][:, 0:1], ALU.mult, ALU.add)
        mk.inc(ve.drain(), "sv", 1, f"qk:{g}")
        for uu in range(16):
            pb = PS[6] if uu % 2 == 0 else PS[7]
            if uu == 0:
                mk.wait(te, f"bsev:{g}")
            if uu == 1 and g > 0:
                mk.wait(te, f"uev:{g - 1}:15")
            if uu >= 2:
                mk.wait(te, f"uev:{g}:{uu - 2}")
            for f in range(8):
                inst = te.matmul(pb[:, 0:512], ab(WUV + f * 4224 + uu * 128, 128),
                                 ab(XNF + f * 512, 512), start=(f == 0), stop=(f == 7))
            mk.inc(inst, "st", 1, f"umm:{g}:{uu}")
            mk.wait(ac, f"umm:{g}:{uu}")
            if uu % 8 == 0 and (g > 0 or uu == 8):
                ph = (g * 2 + uu // 8 - 1)
                mk.wait(ac, f"upark:{ph // 2}:{ph % 2}")
            inst = ac.activation(ab(UGRP + (uu % 8) * 512, 512), pb[:, 0:512],
                                 AF.Silu, bias=C["buvfm_sb"][:, uu:uu + 1])
            mk.inc(inst, "sa", 1, f"uev:{g}:{uu}")
            if uu % 8 == 7:
                h = uu // 8
                mk.inc(ac.drain(), "sa", 1, f"uevd:{g}:{h}")
                mk.wait(ac, f"uevd:{g}:{h}")
                mk.inc(ac.dma_start(out=C["upark"].ap()[g, :, h * 4096:(h + 1) * 4096],
                                    in_=ab(UGRP, 4096)), "da", 16, f"upark:{g}:{h}")

    for _t in range(NT):
        p1_pre(_t)
    mk.wait(gp, "pre:15")
    wout3p = C["wout"].rearrange("(c p) n -> p c n", p=128)
    woutbf3 = C["wout_bf"].ap().rearrange("p (c n) -> p c n", c=16)
    mk.inc(gp.dma_start(out=woutbf3, in_=wout3p), "dg", 16, "c:woutd")
    w13p = C["w1"].rearrange("(c p) n -> p c n", p=128)
    w1bf3 = C["w1_bf"].ap().rearrange("p (c n) -> p c n", c=8)
    mk.inc(gp.dma_start(out=w1bf3, in_=w13p), "dg", 16, "c:w1d")
    p1_newton()
    p1_load(0)
    p1_load(1)
    p1_xn(0)
    mk.inc(ac.drain(), "sa", 1, "xnd:0")
    p1_tr(0)
    mk.wait(ac, "xnd:0")
    mk.inc(ac.dma_start(out=C["xnp_in"].ap()[0:128, :], in_=xnb[0][:]),
           "da", 16, "xnpark:0")
    for t in range(NT):
        p1_body(t)
        if t + 2 < NT:
            p1_load(t + 2)
        if t % 4 == 3:
            p1_tail(t // 4)
            if t + 1 < NT:
                p1_xn(t + 1)
                mk.inc(ac.drain(), "sa", 1, f"xnd:{t + 1}")
                p1_tr(t + 1)
                mk.wait(ac, f"xnd:{t + 1}")
                mk.inc(ac.dma_start(
                    out=C["xnp_in"].ap()[(t + 1) * 128:(t + 2) * 128, :],
                    in_=xnb[0][:]), "da", 16, f"xnpark:{t + 1}")
            k1 = t // 4
            mk.wait(gp, f"xnpark:{t}")
            mk.inc(gp.collective_compute(
                "AllGather", ALU.bypass,
                replica_groups=[list(range(NCORE))],
                ins=[C["xnp_in"].ap()[(t - 3) * 128:(t + 1) * 128, :]],
                outs=[C["xnp"].ap()[k1 * 4096:(k1 + 1) * 4096, :]]),
                "cc1", 1, f"ag1:{k1}")
    mk.wait(ve, "qk:3")
    ve.tensor_scalar(af(RAWVO, 128), af(LGO, 128), 60.0, None, ALU.min)
    mk.inc(ve.drain(), "sv", 1, "lgclamp")
    mk.wait(ac, "lgclamp")
    ac.activation(af(EXPO, 128), af(RAWVO, 128), AF.Exp)
    mk.inc(ac.drain(), "sa", 1, "expall")
    mk.wait(ve, "expall")
    for t in range(NT):
        ve.tensor_reduce(af(T2O + t, 1), af(EXPO + t * 8, 8), AX, ALU.add)
        ve.tensor_reduce(af(MXO + t, 1), af(EXPO + t * 8, 8), AX, ALU.max)
    ve.drain()
    ve.reciprocal(af(T3O, 16), af(T2O, 16))
    ve.drain()
    ve.tensor_mul(af(G0O, 16), af(MXO, 16), af(T3O, 16))
    mk.inc(ve.drain(), "sv", 1, "g0all")
    for t in range(NT):
        ve.tensor_scalar(af(RAWVO + t * 8, 8), af(EXPO + t * 8, 8),
                         af(MXO + t, 1), None, ALU.is_ge)
    ve.drain()
    r3 = af(RAWVO, 128).rearrange("p (t e) -> p t e", e=8)
    p13 = af(PFX1O, 128).rearrange("p (t e) -> p t e", e=8)
    p23 = af(PFX2O, 128).rearrange("p (t e) -> p t e", e=8)
    ve.tensor_copy(p13[:, :, 0:1], r3[:, :, 0:1])
    ve.tensor_add(p13[:, :, 1:8], r3[:, :, 1:8], r3[:, :, 0:7])
    ve.drain()
    ve.tensor_copy(p23[:, :, 0:2], p13[:, :, 0:2])
    ve.tensor_add(p23[:, :, 2:8], p13[:, :, 2:8], p13[:, :, 0:6])
    ve.drain()
    ve.tensor_copy(p13[:, :, 0:4], p23[:, :, 0:4])
    ve.tensor_add(p13[:, :, 4:8], p23[:, :, 4:8], p23[:, :, 0:4])
    ve.drain()
    ve.tensor_sub(af(PFX2O, 128), af(PFX1O, 128), af(RAWVO, 128))
    ve.drain()
    ve.tensor_scalar(af(PFX2O, 128), af(PFX2O, 128), 0.5, None, ALU.is_lt)
    ve.drain()
    ve.tensor_mul(af(M1O, 128), af(RAWVO, 128), af(PFX2O, 128))
    mk.inc(ve.drain(), "sv", 1, "m1all")
    for t in range(NT):
        mk.wait(te, "m1all")
        if t == 0:
            mk.wait(te, "vev:15")
        inst = te.matmul(PS[1][0:1, t * 8:(t + 1) * 8], C["onesc_sb"][:],
                         af(M1O + t * 8, 8), start=True, stop=True)
    mk.inc(inst, "st", 1, "csall")
    mk.wait(ve, "csall")
    ve.tensor_copy(C["csrow_sb"][:], PS[1][0:1, 0:128])
    mk.inc(ve.drain(), "sv", 1, "csrowev")
    cs3 = C["csrow_sb"][:].rearrange("p (t e) -> p t e", e=8)
    a3 = C["rowa_sb"][:].rearrange("p (t e) -> p t e", e=8)
    b3 = C["rowb_sb"][:].rearrange("p (t e) -> p t e", e=8)
    ve.tensor_copy(a3[:, 0:1, :], cs3[:, 0:1, :])
    ve.tensor_add(a3[:, 1:16, :], cs3[:, 1:16, :], cs3[:, 0:15, :])
    ve.drain()
    ve.tensor_copy(b3[:, 0:2, :], a3[:, 0:2, :])
    ve.tensor_add(b3[:, 2:16, :], a3[:, 2:16, :], a3[:, 0:14, :])
    ve.drain()
    ve.tensor_copy(a3[:, 0:4, :], b3[:, 0:4, :])
    ve.tensor_add(a3[:, 4:16, :], b3[:, 4:16, :], b3[:, 0:12, :])
    ve.drain()
    ve.tensor_copy(b3[:, 0:8, :], a3[:, 0:8, :])
    ve.tensor_add(b3[:, 8:16, :], a3[:, 8:16, :], a3[:, 0:8, :])
    ve.drain()
    ve.tensor_sub(C["rowa_sb"][:], C["rowb_sb"][:], C["csrow_sb"][:])
    ve.tensor_copy(C["hist_row_sb"][:], C["rowb_sb"][0:1, 120:128])
    mk.inc(ve.drain(), "sv", 1, "pfxd")
    mk.wait(ac, "pfxd")
    mk.inc(ac.dma_start(out=C["hist_in"].ap(), in_=C["hist_row_sb"][:]),
           "da", 16, "histw")
    for t in range(NT):
        pb = PS[t % 2]
        if t < 2:
            mk.wait(te, "pfxd")
            if t == 1:
                mk.wait(te, "csrowev")
        else:
            mk.wait(te, f"rkev:{t - 2}")
        te.matmul(pb[:, 0:8], C["sut_sb"][:], af(M1O + t * 8, 8),
                  start=True, stop=False)
        mk.inc(te.matmul(pb[:, 0:8], C["ones1f_sb"][:], C["rowa_sb"][0:1, t * 8:(t + 1) * 8],
                         start=False, stop=True), "st", 1, f"rkmm:{t}")
        mk.wait(ve, f"rkmm:{t}")
        mk.inc(ve.tensor_copy(af(RANKO + t * 8, 8), pb[:, 0:8]), "sv", 1, f"rkev:{t}")
    mk.inc(ve.drain(), "sv", 1, "rkevall")

    mk.wait(sy, "umm:3:15")
    mk.wait(sy, "c:woutd")
    mk.inc(sy.dma_start(out=ab(WOUTC, 16 * 1024), in_=C["wout_bf"].ap()),
           "ds", 16, "woutld0")
    mk.wait(sy, "woutld0")
    mk.inc(sy.dma_start(out=C["fence_b"].ap()[:, 0:512], in_=ab(WOUTC, 512)),
           "ds", 16, "woutld")
    for qb in range(NG):
        for uh in range(2):
            if qb > 0:
                mk.wait(gp, f"uavmul:{qb - 1}:{8 * uh + 7}")
            if qb == 0 and uh == 0:
                mk.wait(gp, "upark:3:1")
            mk.inc(gp.dma_start(out=ab(USBC + uh * 4096, 4096),
                                in_=C["upark"].ap()[qb, :, uh * 4096:(uh + 1) * 4096]),
                   "dg", 16, f"uld:{qb}:{uh}")
        if qb == 0:
            mk.wait(gp, "histw")
            mk.inc(gp.collective_compute("AllGather", ALU.bypass,
                                         replica_groups=[list(range(NCORE))],
                                         ins=[C["hist_in"].ap()], outs=[C["hist_out"].ap()]),
                   "cc2", 1, "aghist")
        if qb == 1:
            mk.wait_sem(gp, "cc2", 1)
            mk.inc(gp.dma_start(out=C["hist_sb"][:], in_=C["hist_out"].ap()),
                   "dg", 16, "histr")
        if qb == 2:
            mk.wait(gp, "idxu")
            mk.wait(gp, "c:tzero")
            for t in range(NT):
                mk.inc(gp.indirect_dma_start(
                    out=C["tabl"].ap(),
                    out_offset=IndirectOffsetOnAxis(C["didx_sb"][:, t:t + 1], 0),
                    in_=C["tokv_sb"][:, t:t + 1], in_offset=None,
                    bounds_check=rb2, oob_is_err=False), "dg", 16, f"tsc:{t}")
            mk.wait(gp, "tsc:15")
            mk.inc(gp.dma_start(out=C["fence_f"].ap(),
                                in_=C["tabl"].ap()[0:16384, 0:1].rearrange("(a b) c -> a (b c)", a=128)),
                   "dg", 16, "tscf")
            mk.wait(gp, "tscf")
            mk.inc(gp.collective_compute("AllReduce", ALU.add,
                                         replica_groups=[list(range(NCORE))],
                                         ins=[C["tabl"].ap()], outs=[C["tabl_red"].ap()]),
                   "cc3", 1, "artab")
        for mt in range(NT):
            pb = PS[mt % 2]
            if qb == 0 and mt < 2:
                mk.wait(te, "qk:3")
                mk.wait(te, "rkevall")
            if qb > 0 and mt < 2:
                mk.wait(te, f"prtadd:{4 * qb - 1}")
            if qb * NT + mt >= 2:
                p = qb * NT + mt - 2
                mk.wait(te, f"srel:{p // NT}:{p % NT}")
            mk.inc(te.matmul(pb[:, 0:512], ab(KBF + mt * 128, 128),
                             ab(QBF + qb * 512, 512), start=True, stop=True),
                   "st", 1, f"smm:{qb}:{mt}")
            mk.wait(ve, f"smm:{qb}:{mt}")
            scr = af(SCRO + (mt % 2) * 512, 512)
            mk.inc(ve.tensor_scalar(scr, pb[:, 0:512], 0.0, None, ALU.max),
                   "sv", 1, f"srel:{qb}:{mt}")
            mk.wait(ac, f"srel:{qb}:{mt}")
            mk.inc(ac.activation(ab(SQTC + mt * 512, 512), scr, AF.Square,
                                 scale=1.0 / TOK), "sa", 1, f"sqev:{qb}:{mt}")
        for et in range(NT):
            pb = PS[2 + et % 2]
            if qb == 2 and et < 2:
                mk.wait(te, "ofsbev")
            if qb * NT + et >= 2:
                p = qb * NT + et - 2
                mk.wait(te, f"uavmul:{p // NT}:{p % NT}")
            mk.wait(te, f"sqev:{qb}:15")
            for mt in range(NT):
                inst = te.matmul(pb[:, 0:512],
                                 ab(V_TM + mt * 2048 + et * 128, 128),
                                 ab(SQTC + mt * 512, 512),
                                 start=(mt == 0), stop=(mt == NT - 1))
            mk.inc(inst, "st", 1, f"avmm:{qb}:{et}")
            mk.wait(ve, f"avmm:{qb}:{et}")
            mk.wait(ve, f"uld:{qb}:{et // 8}")
            mk.inc(ve.tensor_mul(ab(UAVC + et * 512, 512), pb[:, 0:512],
                                 ab(USBC + et * 512, 512)), "sv", 1, f"uavmul:{qb}:{et}")
        mk.inc(None, "sv", 0)
        if mk.record:
            mk.marks[f"uavall:{qb}"] = ("sv", mk.cnt["sv"])
        for s2 in range(4):
            t4 = qb * 4 + s2
            pr = (PS[6], PS[7]) if s2 % 2 == 0 else (PS[0], PS[1])
            mk.wait(te, f"uavmul:{qb}:15")
            if qb == 0 and s2 == 0:
                mk.wait(te, "woutld")
                mk.wait(te, "uev:3:15")
            if qb * 4 + s2 >= 2:
                p = qb * 4 + s2 - 2
                mk.wait(te, f"prtadd:{p}")
            te.matmul(pr[0][:, 0:512], C["ones1b_sb"][:], C["boutr_sb"][0:1, 0:512],
                      start=True, stop=False)
            te.matmul(pr[1][:, 0:512], C["ones1b_sb"][:], C["boutr_sb"][0:1, 512:1024],
                      start=True, stop=False)
            for et in range(NT):
                lhs = ab(UAVC + et * 512 + s2 * 128, 128)
                te.matmul(pr[0][:, 0:512], lhs, ab(WOUTC + et * 1024, 512),
                          start=False, stop=(et == NT - 1))
                inst = te.matmul(pr[1][:, 0:512], lhs, ab(WOUTC + et * 1024 + 512, 512),
                                 start=False, stop=(et == NT - 1))
            mk.inc(inst, "st", 1, f"omm:{qb}:{s2}")
            if t4 < 2:
                mk.wait(sy, f"xn:{14 + t4}")
            else:
                mk.wait(sy, f"prtadd:{t4 - 2}")
            mk.inc(sy.dma_start(out=af(XFO + (t4 % 2) * 1024, 1024),
                                in_=x_ap[t4 * 128:(t4 + 1) * 128, :]),
                   "ds", 16, f"xf4:{t4}")
            mk.wait(ve, f"omm:{qb}:{s2}")
            mk.wait(ve, f"xf4:{t4}")
            if t4 >= 2:
                mk.wait(ve, f"prtw:{t4 - 2}")
            xf4 = af(XFO + (t4 % 2) * 1024, 1024)
            prt = af(PRTO + (t4 % 2) * 1024, 1024)
            ve.tensor_add(prt[:, 0:512], pr[0][:, 0:512], xf4[:, 0:512])
            ve.tensor_add(prt[:, 512:1024], pr[1][:, 0:512], xf4[:, 512:1024])
            mk.inc(ve.drain(), "sv", 1, f"prtadd:{t4}")
            mk.wait(sy, f"prtadd:{t4}")
            mk.inc(sy.dma_start(out=C["partial"].ap()[t4 * 128:(t4 + 1) * 128, :],
                                in_=prt), "ds", 16, f"prtw:{t4}")
        if qb == 1 and s2 == 3:
            mk.wait(te, "histr")
            mk.wait(te, "uavmul:1:15")
            mk.inc(te.matmul(PS[2][0:1, 0:8], C["psel_sb"][:], C["hist_sb"][:],
                             start=True, stop=True), "st", 1, "offmm")
            mk.wait(ve, "offmm")
            ve.tensor_copy(C["offsr_sb"][:], PS[2][0:1, 0:8])
            mk.inc(ve.drain(), "sv", 1, "offev")
            mk.wait(te, "offev")
            mk.inc(te.matmul(PS[3][0:128, 0:8], C["ones1f_sb"][:], C["offsr_sb"][:],
                             start=True, stop=True), "st", 1, "ofbmm")
            mk.wait(ve, "ofbmm")
            ve.tensor_copy(af(OFSO, 8), PS[3][:, 0:8])
            mk.inc(ve.drain(), "sv", 1, "ofsbev")
            ve.drain()
            for t in range(NT):
                ve.tensor_add(af(RAWVO + t * 8, 8), af(RANKO + t * 8, 8), af(OFSO, 8))
            ve.drain()
            ve.tensor_mul(af(RAWVO, 128), af(RAWVO, 128), af(M1O, 128))
            ve.tensor_mul(af(PFX1O, 128), af(M1O, 128), C["iota8_sb"][:])
            ve.drain()
            for t in range(NT):
                ve.tensor_reduce(af(LOCO + t, 1), af(RAWVO + t * 8, 8), AX, ALU.add)
                ve.tensor_reduce(af(ESO + t, 1), af(PFX1O + t * 8, 8), AX, ALU.add)
            ve.drain()
            ve.tensor_scalar(af(KEPTO, 16), af(LOCO, 16), float(CAP), None, ALU.is_lt)
            ve.drain()
            ve.tensor_mul(af(GFINO, 16), af(G0O, 16), af(KEPTO, 16))
            mk.inc(ve.drain(), "sv", 1, "gfin")
            mk.wait(ac, "gfin")
            ac.copy(C["loci_sb"][:], af(LOCO, 16))
            mk.inc(ac.drain(), "sa", 1, "loci")
            mk.wait(ve, "loci")
            ve.tensor_scalar(C["hi_sb"][:], C["loci_sb"][:], 7, None, ALU.arith_shift_right)
            ve.tensor_scalar(C["lo_sb"][:], C["loci_sb"][:], 127, None, ALU.bitwise_and)
            ve.drain()
            ve.scalar_tensor_tensor(af(SGMO, 16), C["lo_sb"][:], 16.0, C["hi_sb"][:],
                                    ALU.mult, ALU.add)
            ve.tensor_scalar(af(PFX2O, 16), af(LOCO, 16), 1536.0, None, ALU.is_ge)
            ve.scalar_tensor_tensor(af(RAWVO, 16), af(ESO, 16), 1536.0, af(LOCO, 16),
                                    ALU.mult, ALU.add)
            ve.scalar_tensor_tensor(af(INVO, 16), af(ESO, 16), 512.0, af(LOCO, 16),
                                    ALU.mult, ALU.add)
            ve.drain()
            ve.tensor_scalar(af(INVO, 16), af(INVO, 16), 1.0, 10752.0, ALU.mult, ALU.add)
            ve.drain()
            ve.tensor_sub(af(INVO, 16), af(INVO, 16), af(RAWVO, 16))
            ve.drain()
            ve.tensor_mul(af(INVO, 16), af(INVO, 16), af(PFX2O, 16))
            ve.drain()
            ve.tensor_add(af(MXO, 16), af(RAWVO, 16), af(INVO, 16))
            ve.tensor_scalar(af(T1O, 16), af(ESO, 16), float(CAP), None, ALU.mult)
            ve.tensor_scalar(af(T3O, 16), af(KEPTO, 16), -1.0, 1.0, ALU.mult, ALU.add)
            ve.drain()
            ve.tensor_add(af(T2O, 16), af(T1O, 16), af(SGMO, 16))
            ve.tensor_copy(af(T1O, 16), af(MXO, 16))
            ve.tensor_mul(af(PFX1O, 16), af(T3O, 16), C["tokv_sb"][:])
            ve.tensor_scalar(af(PFX2O, 16), af(T3O, 16), BIGF, None, ALU.mult)
            ve.tensor_scalar(af(PFX2O + 16, 16), af(T3O, 16), float(NEXP * CAP), None, ALU.mult)
            ve.drain()
            ve.tensor_mul(af(T2O, 16), af(T2O, 16), af(KEPTO, 16))
            ve.tensor_mul(af(T1O, 16), af(T1O, 16), af(KEPTO, 16))
            ve.tensor_add(af(PFX2O + 16, 16), af(PFX1O, 16), af(PFX2O + 16, 16))
            ve.tensor_add(af(PFX1O, 16), af(PFX1O, 16), af(PFX2O, 16))
            ve.drain()
            ve.tensor_add(af(DIDXO, 16), af(T2O, 16), af(PFX2O + 16, 16))
            ve.tensor_add(af(CIDXO, 16), af(T1O, 16), af(PFX1O, 16))
            ve.drain()
            ve.tensor_scalar(af(T3O, 16), af(CIDXO, 16), 12288.0, None, ALU.is_lt)
            ve.drain()
            ve.scalar_tensor_tensor(af(MXO, 16), af(T3O, 16), BIGF, af(CIDXO, 16),
                                    ALU.mult, ALU.add)
            mk.inc(ve.drain(), "sv", 1, "cidxf")
            mk.wait(ac, "cidxf")
            ac.copy(C["didx_sb"][:], af(DIDXO, 16))
            ac.copy(C["cidx_sb"][:], af(CIDXO, 16))
            ac.copy(C["cidx2_sb"][:], af(MXO, 16))
            mk.inc(ac.drain(), "sa", 1, "idxu")


    mk.wait_sem(gp, "cc3", 1)
    mk.inc(gp.indirect_dma_start(
        out=af(INVO, 16), out_offset=None,
        in_=C["tabl_red"].ap(),
        in_offset=IndirectOffsetOnAxis(C["sgbase_sb"][:, 0:1], 0)),
        "dg", 16, "invg0")
    mk.wait(gp, "invg0")
    mk.inc(gp.dma_start(out=C["fence_f"].ap()[:, 0:16], in_=af(INVO, 16)), "dg", 16, "invg")
    mk.wait(ve, "invg")
    ve.tensor_scalar(af(T1O, 16), af(INVO, 16), 0.5, None, ALU.is_lt)
    ve.drain()
    ve.scalar_tensor_tensor(af(T2O, 16), af(T1O, 16), BIGF, af(INVO, 16),
                            ALU.mult, ALU.add)
    ve.drain()
    ve.tensor_scalar(af(SIDXO, 16), af(T2O, 16), 1.0, -1.0, ALU.mult, ALU.add)
    mk.inc(ve.drain(), "sv", 1, "sidxf")
    mk.wait(ac, "sidxf")
    ac.copy(C["loci_sb"][:], af(SIDXO, 16))
    mk.inc(ac.drain(), "sa", 1, "sidxi")
    mk.wait(ve, "sidxi")
    ve.tensor_scalar(C["hi_sb"][:], C["loci_sb"][:], 9, None, ALU.arith_shift_right)
    ve.tensor_scalar(C["k9_sb"][:], C["loci_sb"][:], 11, None, ALU.arith_shift_right)
    ve.tensor_scalar(C["l9_sb"][:], C["loci_sb"][:], 511, None, ALU.bitwise_and)
    ve.drain()
    ve.tensor_scalar(C["lo_sb"][:], C["hi_sb"][:], 3, None, ALU.bitwise_and)
    ve.drain()
    ve.scalar_tensor_tensor(af(T1O, 16), C["k9_sb"][:], 512.0, C["l9_sb"][:],
                            ALU.mult, ALU.add)
    ve.drain()
    ve.scalar_tensor_tensor(af(SIDXO, 16), C["lo_sb"][:], 4096.0, af(T1O, 16),
                            ALU.mult, ALU.add)
    mk.inc(ve.drain(), "sv", 1, "sidxf2")
    mk.wait(ac, "sidxf2")
    ac.copy(C["sidx_sb"][:], af(SIDXO, 16))
    mk.inc(ac.drain(), "sa", 1, "sidxu")

    ve.memset(C["slots0"][:], 0.0)
    ve.memset(C["slots1"][:], 0.0)
    mk.inc(ve.drain(), "sv", 1, "slz")
    slots = [C["slots0"], C["slots1"]]
    eo_pairs = [(PS[0], PS[1]), (PS[6], PS[7])]
    last_pair_mark = [None, None]
    for b in range(4):
        for sp in range(2):
            for st in (2 * sp, 2 * sp + 1):
                gi = b * 4 + st
                mk.wait(gp, "sidxu")
                mk.wait_sem(gp, "cc1", 4)
                if gi == 0:
                    mk.wait(gp, "slz")
                if gi >= 2:
                    p = gi - 2
                    mk.wait(gp, f"stev:{p // 4}:{p % 4}:7")
                mk.inc(gp.indirect_dma_start(
                    out=slots[st % 2][:], out_offset=None,
                    in_=C["xnp"].ap(),
                    in_offset=IndirectOffsetOnAxis(C["sidx_sb"][:, gi:gi + 1], 0),
                    bounds_check=rbc, oob_is_err=False),
                    "dg", 16, f"sg:{b}:{st}")
            st1 = 2 * sp + 1
            mk.wait(gp, f"sg:{b}:{st1}")
            mk.inc(gp.dma_start(
                out=(C["d_slots"].ap() if (debug and b == 0 and sp == 0) else
                     C["fence_b"].ap()),
                in_=slots[1][:]), "dg", 16, f"sgf:{b}:{st1}")
            for st in (2 * sp, 2 * sp + 1):
                gi = b * 4 + st
                mk.wait(te, f"sgf:{b}:{st1}")
                for jj in range(8):
                    k = gi * 8 + jj
                    if k >= 2:
                        pk = k - 2
                        mk.wait(te, f"stev:{pk // 32}:{(pk % 32) // 8}:{pk % 8}")
                    inst = te.transpose(PT[jj % 2][:, 0:128],
                                        slots[st % 2][:, jj * 128:(jj + 1) * 128],
                                        C["ident_sb"][:])
                    mk.inc(inst, "st", 1, f"sttrj:{b}:{st}:{jj}")
                    mk.wait(ve, f"sttrj:{b}:{st}:{jj}")
                    mk.inc(ve.tensor_copy(ab(SFMC + jj * 512 + st * 128, 128),
                                          PT[jj % 2][:, 0:128]),
                           "sv", 1, f"stev:{b}:{st}:{jj}")
                mk.inc(None, "st", 0)
                if mk.record:
                    mk.marks[f"sttr:{b}:{st}"] = ("st", mk.cnt["st"])
        if b == 0:
            mk.wait(ac, "avmm:3:15")
            mk.wait(ac, "c:w1d")
            mk.inc(ac.dma_start(out=ab(W1C, 8 * 4096), in_=C["w1_bf"].ap()),
                   "da", 16, "w1ld0")
            mk.wait(ac, "w1ld0")
            mk.inc(ac.dma_start(out=C["fence_b"].ap()[:, 0:512], in_=ab(W1C, 512)),
                   "da", 16, "w1ld:3")
            mk.wait(gp, "omm:3:3")
            w23 = C["w2"].rearrange("(c p) n -> p c n", p=128)
            w2sb3 = ab(W2C, 32 * 1024).rearrange("p (c n) -> p c n", c=32)
            for wc in range(2):
                mk.inc(gp.dma_start(out=w2sb3[:, 16 * wc:16 * wc + 16, :],
                                    in_=w23[:, 16 * wc:16 * wc + 16, :]), "dg", 16)
                mk.inc(gp.dma_start(out=C["fence_b"].ap()[:, 0:512],
                                    in_=ab(W2C + 16 * wc * 1024, 512)),
                       "dg", 16, f"w2ld:{wc}")
        for h2 in range(2):
            for ft in range(32):
                pb = PS[2 + ft % 2]
                if b == 0 and h2 == 0 and ft < 2:
                    mk.wait(te, "w1ld:3")
                    mk.wait(te, "uavmul:3:15")
                if h2 == 0 and ft == 0:
                    mk.wait(te, f"stev:{b}:3:7")
                k = (b * 2 + h2) * 32 + ft
                if k >= 2:
                    pk = k - 2
                    mk.wait(te, f"hev:{pk // 64}:{(pk % 64) // 32}:{pk % 32}")
                for m in range(8):
                    inst = te.matmul(pb[:, 0:256],
                                     ab(W1C + m * 4096 + ft * 128, 128),
                                     ab(SFMC + m * 512 + h2 * 256, 256),
                                     start=(m == 0), stop=(m == 7))
                mk.inc(inst, "st", 1, f"hmm:{b}:{h2}:{ft}")
                mk.wait(ac, f"hmm:{b}:{h2}:{ft}")
                if b == 0 and h2 == 0 and ft == 0:
                    mk.wait(ac, "upark:3:1")
                inst = ac.activation(ab(HBUFC + ft * 256, 256), pb[:, 0:256],
                                     AF.Gelu_apprx_tanh, bias=C["b1fm_sb"][:, ft:ft + 1])
                mk.inc(inst, "sa", 1, f"hev:{b}:{h2}:{ft}")
            for sub in range(2):
                pi = sub
                pr = eo_pairs[pi]
                srow = b * 512 + h2 * 256 + sub * 128
                mk.wait(te, f"hev:{b}:{h2}:31")
                if last_pair_mark[pi] is not None:
                    mk.wait(te, last_pair_mark[pi])
                elif pi == 0:
                    mk.wait(te, "prtadd:15")
                else:
                    mk.wait(te, "prtadd:14")
                mk.wait(te, "w2ld:1" if sub or True else "w2ld:0")
                te.matmul(pr[0][:, 0:512], C["ones1b_sb"][:], C["b2r_sb"][0:1, 0:512],
                          start=True, stop=False)
                te.matmul(pr[1][:, 0:512], C["ones1b_sb"][:], C["b2r_sb"][0:1, 512:1024],
                          start=True, stop=False)
                for ft in range(32):
                    lhs = ab(HBUFC + ft * 256 + sub * 128, 128)
                    te.matmul(pr[0][:, 0:512], lhs, ab(W2C + ft * 1024, 512),
                              start=False, stop=(ft == 31))
                    inst = te.matmul(pr[1][:, 0:512], lhs, ab(W2C + ft * 1024 + 512, 512),
                                     start=False, stop=(ft == 31))
                mk.inc(inst, "st", 1, f"eomm:{b}:{h2}:{sub}")
                eb = C["eob0"] if (srow // 128) % 2 == 0 else C["eob1"]
                mk.wait(ve, f"eomm:{b}:{h2}:{sub}")
                if srow // 128 >= 2:
                    mk.wait(ve, f"eow:{srow // 128 - 2}")
                ve.tensor_copy(eb[:, 0:512], pr[0][:, 0:512])
                ve.tensor_copy(eb[:, 512:1024], pr[1][:, 0:512])
                mk.inc(ve.drain(), "sv", 1, f"eoev:{b}:{h2}:{sub}")
                last_pair_mark[pi] = f"eoev:{b}:{h2}:{sub}"
                mk.wait(sy, f"eoev:{b}:{h2}:{sub}")
                mk.inc(sy.dma_start(out=C["eo_in"].ap()[srow:srow + 128, :], in_=eb[:]),
                       "ds", 16, f"eow:{srow // 128}")
                if srow in (1408, 1920):
                    kch = 0 if srow == 1408 else 1
                    lo5, hi5 = (0, 1536) if kch == 0 else (1536, 2048)
                    plo, phi = (0, 12288) if kch == 0 else (12288, 16384)
                    if kch == 1:
                        mk.wait_sem(gp, "c40", 1)
                        mk.wait(gp, "hmm:3:1:31")
                        for t5 in range(NT):
                            mk.inc(gp.indirect_dma_start(
                                out=ab(W1C + t5 * 1024, 1024), out_offset=None,
                                in_=C["eo_pool"].ap(),
                                in_offset=IndirectOffsetOnAxis(
                                    C["cidx_sb"][:, t5:t5 + 1], 0),
                                bounds_check=rbc, oob_is_err=False),
                                "dg", 16, f"fg0:{t5}")
                    mk.wait(gp, f"eow:{hi5 // 128 - 1}")
                    mk.inc(gp.collective_compute(
                        "AllGather", ALU.bypass,
                        replica_groups=[list(range(NCORE))],
                        ins=[C["eo_in"].ap()[lo5:hi5, :]],
                        outs=[C["eo_pool"].ap()[plo:phi, :]]),
                        f"c4{kch}", 1, f"ag3:{kch}")


    for tp_ in range(8):
        for t in (2 * tp_, 2 * tp_ + 1):
            if t == 0:
                mk.wait_sem(gp, "c41", 1)
            mk.inc(gp.indirect_dma_start(
                out=ab(W1C + t * 1024, 1024), out_offset=None,
                in_=C["eo_pool"].ap(),
                in_offset=IndirectOffsetOnAxis(C["cidx2_sb"][:, t:t + 1], 0),
                bounds_check=rbc, oob_is_err=False),
                "dg", 16, f"fg1:{t}")
        t1 = 2 * tp_ + 1
        mk.wait(gp, f"fg1:{t1}")
        mk.inc(gp.dma_start(out=C["fence_b"].ap(), in_=ab(W1C + t1 * 1024, 1024)),
               "dg", 16, f"fg:{t1}")
        for t in (2 * tp_, 2 * tp_ + 1):
            if t == 0:
                mk.wait(ac, "prtw:15")
            if t >= 2:
                mk.wait(ac, f"fmul:{t - 2}")
            prt = af(PRTO + (t % 2) * 1024, 1024)
            mk.inc(ac.dma_start(out=prt, in_=C["partial"].ap()[t * 128:(t + 1) * 128, :]),
                   "da", 16, f"pld:{t}")
            mk.wait(ve, f"fg:{t1}")
            mk.wait(ve, f"pld:{t}")
            if t >= 2:
                mk.wait(ve, f"yw:{t - 2}")
            outb = af(XFO + (t % 2) * 1024, 1024)
            ve.scalar_tensor_tensor(outb, ab(W1C + t * 1024, 1024), af(GFINO + t, 1),
                                    prt, ALU.mult, ALU.add)
            mk.inc(ve.drain(), "sv", 1, f"fmul:{t}")
            mk.wait(sy, f"fmul:{t}")
            mk.inc(sy.dma_start(out=C["y"].ap()[t * 128:(t + 1) * 128, :], in_=outb),
                   "ds", 16, f"yw:{t}")
    if debug:
        taps = [
            ("d_mu", af(MUO, 16), "gfin"),
            ("d_vv", af(VVO, 16), "gfin"),
            ("d_rstd", af(RSTDO, 16), "gfin"),
            ("d_sumx2", af(SUMX2, 16), "gfin"),
            ("d_xf", af(XFO, 2048), "yw:15"),
            ("d_lg", af(LGO, 128), "gfin"),
            ("d_m1", af(M1O, 128), "gfin"),
            ("d_rank", af(RANKO, 128), "gfin"),
            ("d_g0", af(G0O, 16), "gfin"),
            ("d_loc", af(LOCO, 16), "gfin"),
            ("d_kept", af(KEPTO, 16), "gfin"),
            ("d_es", af(ESO, 16), "cidxf"),
            ("d_gfin", af(GFINO, 16), "gfin"),
            ("d_didx", af(DIDXO, 16), "cidxf"),
            ("d_cidx", af(CIDXO, 16), "cidxf"),
            ("d_inv", af(INVO, 16), "sidxf"),
            ("d_sidx", af(SIDXO, 16), "sidxf"),
            ("d_q", ab(QBF, 2048), "qk:3"),
            ("d_k", ab(KBF, 2048), "qk:3"),
            ("d_vtm", ab(V_TM, 32768), "vev:15"),
            ("d_sqt", ab(SQTC, 8192), "sqev:3:15"),
        ]
        for name, src, gate in taps:
            mk.wait(sy, gate)
            mk.wait(sy, "yw:15")
            mk.inc(sy.dma_start(out=C[name].ap(), in_=src), "ds", 16, f"tap:{name}")
        for name, src in [("d_xnp", C["xnp_in"]), ("d_partial", C["partial"]),
                          ("d_eoin", C["eo_in"]), ("d_tab", C["tabl_red"]),
                          ("d_hist", C["hist_out"]), ("d_upark", C["upark"])]:
            mk.wait(gp, "yw:15")
            mk.inc(gp.dma_start(out=C[name].ap(), in_=src.ap()), "dg", 16, f"tap:{name}")

    for eng, sem in ((sy, "ds"), (gp, "dg"), (ac, "da")):
        if mk.record:
            mk.marks.setdefault(f"fin:{sem}", (sem, mk.cnt.get(sem, 0)))
        mk.wait(eng, f"fin:{sem}")


def build(debug=False):
    nc = bass.Bass("TRN2", target_bir_lowering=False)
    C = {}

    def inp(name, shape, dt=F32):
        h = nc.dram_tensor(name, shape, dt, kind="ExternalInput")
        return h.ap()

    C["x"] = inp("x", [TOK, D])
    C["wuv"] = inp("wuv", [D, 2 * E2 + SDIM])
    C["buv_row"] = inp("buv_row", [1, 2 * E2 + SDIM])
    C["wout"] = inp("wout", [E2, D])
    C["boutr"] = inp("boutr", [1, D])
    C["wg"] = inp("wg", [D, NEXP])
    C["w1"] = inp("w1", [D, FF])
    C["w2"] = inp("w2", [FF, D])
    C["b2r"] = inp("b2r", [1, D])
    in_sb = [
        ("ident", [128, 128], BF16), ("sut", [128, 128], F32),
        ("onesc", [128, 1], F32), ("ones1f", [1, 128], F32), ("ones1b", [1, 128], BF16),
        ("iota8", [128, 128], F32), ("psel", [8, 1], F32), ("tokv", [128, 16], F32),
        ("sgbase", [128, 1], U32), ("buvfm", [128, 33], F32), ("b1fm", [128, 32], F32),
        ("gq", [128, 1], F32), ("gk", [128, 1], F32), ("bq", [128, 1], F32),
        ("bk", [128, 1], F32),
    ]
    for name, shape, dt in in_sb:
        C[name + "_d"] = inp(name, shape, dt)
    C["zero_d"] = inp("zero", [128, 128], F32)

    C["y"] = nc.dram_tensor("y", [TOK, D], F32, kind="ExternalOutput")
    if debug:
        for name, shape, dt in [
            ("d_mu", [128, 16], F32), ("d_vv", [128, 16], F32),
            ("d_rstd", [128, 16], F32), ("d_sumx2", [128, 16], F32),
            ("d_xf", [128, 2048], F32),
            ("d_lg", [128, 128], F32), ("d_m1", [128, 128], F32),
            ("d_rank", [128, 128], F32), ("d_g0", [128, 16], F32),
            ("d_loc", [128, 16], F32), ("d_kept", [128, 16], F32),
            ("d_es", [128, 16], F32), ("d_gfin", [128, 16], F32),
            ("d_didx", [128, 16], F32), ("d_cidx", [128, 16], F32),
            ("d_inv", [128, 16], F32), ("d_sidx", [128, 16], F32),
            ("d_q", [128, 2048], BF16), ("d_k", [128, 2048], BF16),
            ("d_vtm", [128, 32768], BF16), ("d_sqt", [128, 8192], BF16),
            ("d_xnp", [TOK, D], BF16), ("d_partial", [TOK, D], F32),
            ("d_eoin", [TOK, D], BF16), ("d_tab", [TABN, 1], F32),
            ("d_hist", [8, 8], F32), ("d_upark", [NG, 128, 8192], BF16),
            ("d_slots", [128, 1024], BF16),
        ]:
            C[name] = nc.dram_tensor(name, shape, dt, kind="ExternalOutput")

    C["xnp_in"] = nc.dram_tensor("xnp_in", [TOK, D], BF16)
    C["xnp"] = nc.dram_tensor("xnp", [NCORE * TOK, D], BF16)
    C["hist_in"] = nc.dram_tensor("hist_in", [1, 8], F32)
    C["hist_out"] = nc.dram_tensor("hist_out", [8, 8], F32)
    C["tabl"] = nc.dram_tensor("tabl", [TABN, 1], F32)
    C["tabl_red"] = nc.dram_tensor("tabl_red", [TABN, 1], F32)
    C["upark"] = nc.dram_tensor("upark", [NG, 128, 8192], BF16)
    C["partial"] = nc.dram_tensor("partial", [TOK, D], F32)
    C["eo_in"] = nc.dram_tensor("eo_in", [TOK, D], BF16)
    C["eo_pool"] = nc.dram_tensor("eo_pool", [NCORE * TOK, D], BF16)
    C["wout_bf"] = nc.dram_tensor("wout_bf", [128, 16 * 1024], BF16)
    C["w1_bf"] = nc.dram_tensor("w1_bf", [128, 32 * 1024], BF16)
    C["fence_b"] = nc.dram_tensor("fence_b", [128, 1024], BF16)
    C["fence_f"] = nc.dram_tensor("fence_f", [128, 128], F32)

    with ExitStack() as ctx:
        e = ctx.enter_context
        C["AB"] = e(nc.sbuf_tensor("AB", [128, AB_COLS], BF16))[:]
        C["AFa"] = e(nc.sbuf_tensor("AFa", [128, AF_COLS], F32))[:]
        for name, shape, dt in in_sb:
            C[name + "_sb"] = e(nc.sbuf_tensor(name + "_sb", shape, dt))
        for name, shape, dt in [
            ("xnb0", [128, 1024], BF16),
            ("xnb1", [128, 1024], BF16), ("slots0", [128, 1024], BF16),
            ("slots1", [128, 1024], BF16), ("ffnb0", [128, 1024], BF16),
            ("ffnb1", [128, 1024], BF16), ("eob0", [128, 1024], BF16),
            ("eob1", [128, 1024], BF16), ("wgb_sb", [128, 64], BF16),
            ("buvv_sb", [1, 2048], BF16), ("boutr_sb", [1, 1024], BF16),
            ("b2r_sb", [1, 1024], BF16), ("csrow_sb", [1, 128], F32),
            ("rowa_sb", [1, 128], F32), ("rowb_sb", [1, 128], F32),
            ("hist_row_sb", [1, 8], F32),
            ("hist_sb", [8, 8], F32), ("offsr_sb", [1, 8], F32),
            ("didx_sb", [128, 16], U32), ("cidx_sb", [128, 16], U32),
            ("cidx2_sb", [128, 16], U32),
            ("sidx_sb", [128, 16], U32), ("loci_sb", [128, 16], I32),
            ("hi_sb", [128, 16], I32), ("lo_sb", [128, 16], I32),
            ("k9_sb", [128, 16], I32), ("l9_sb", [128, 16], I32),
        ]:
            C[name] = e(nc.sbuf_tensor(name, shape, dt))
        PS = {}
        for i in [0, 1, 2, 3]:
            PS[i] = e(nc.psum_tensor(f"PS{i}", [128, 512], F32))
        PT = [e(nc.psum_tensor("PT0", [128, 1024], BF16)),
              e(nc.psum_tensor("PT1", [128, 1024], BF16))]
        PS[6] = e(nc.psum_tensor("PS6", [128, 512], F32))
        PS[7] = e(nc.psum_tensor("PS7", [128, 512], F32))
        C["PS"] = PS
        C["PT"] = PT

        mk = Marks()
        sems = {}
        for s in ["ds", "dg", "da", "sv", "sa", "st",
                  "cc1", "cc2", "cc3", "c40", "c41", "c42", "c43"]:
            sems[s] = e(nc.semaphore(s))
        mk.sems = sems

        fake = _Fake()
        names = ["sy", "gp", "ac", "ve", "te"]
        E = {n: fake for n in names}
        _emit(E, mk, C, debug)
        mk.record = False
        block = e(nc.Block())

        def mk_section(name):
            def section(eng):
                E = {n: (eng if n == name else fake) for n in names}
                _emit(E, mk, C, debug)
            return section

        block.sync(mk_section("sy"))
        block.gpsimd(mk_section("gp"))
        block.scalar(mk_section("ac"))
        block.vector(mk_section("ve"))
        block.tensor(mk_section("te"))
    return nc


def _host_inputs(L0_attn_inputs, L0_attn_W_uv, L0_attn_b_uv, L0_attn_gamma,
                 L0_attn_beta, L0_attn_W_out, L0_attn_b_out, L0_ffn_wg,
                 L0_ffn_w1, L0_ffn_b1, L0_ffn_w2, L0_ffn_b2):
    f32 = np.float32
    x = np.ascontiguousarray(np.asarray(L0_attn_inputs, dtype=f32))
    wuv = np.ascontiguousarray(np.asarray(L0_attn_W_uv, dtype=f32))
    buv = np.asarray(L0_attn_b_uv, dtype=f32)
    gamma = np.asarray(L0_attn_gamma, dtype=f32)
    beta = np.asarray(L0_attn_beta, dtype=f32)
    wout = np.ascontiguousarray(np.asarray(L0_attn_W_out, dtype=f32))
    bout = np.asarray(L0_attn_b_out, dtype=f32)
    wg = np.ascontiguousarray(np.asarray(L0_ffn_wg, dtype=f32))
    w1 = np.ascontiguousarray(np.asarray(L0_ffn_w1, dtype=f32))
    b1 = np.asarray(L0_ffn_b1, dtype=f32)
    w2 = np.ascontiguousarray(np.asarray(L0_ffn_w2, dtype=f32))
    b2 = np.asarray(L0_ffn_b2, dtype=f32)

    ident = np.eye(128, dtype=ml_dtypes.bfloat16)
    sut = np.triu(np.ones((128, 128), f32), 1)
    onesc = np.ones((128, 1), f32)
    ones1f = np.ones((1, 128), f32)
    ones1b = np.ones((1, 128), dtype=ml_dtypes.bfloat16)
    iota8 = np.tile(np.arange(8, dtype=f32), (128, 16))
    buvfm = np.ascontiguousarray(buv.reshape(33, 128).T)
    b1fm_all = b1.reshape(NCORE, 32, 128)
    zero = np.zeros((128, 128), f32)

    maps = []
    for c in range(NCORE):
        tokv = (c * TOK + np.arange(16)[None, :] * 128
                + np.arange(128)[:, None] + 1).astype(f32)
        sgbase = (c * CAP + np.arange(128) * 16).astype(np.uint32)[:, None]
        psel = (np.arange(8) < c).astype(f32)[:, None]
        m = {
            "x": x[c], "wuv": wuv, "buv_row": buv[None, :], "wout": wout,
            "boutr": bout[None, :], "wg": wg, "w1": w1[c],
            "w2": w2[c], "b2r": b2[c][None, :],
            "ident": ident, "sut": sut, "onesc": onesc, "ones1f": ones1f,
            "ones1b": ones1b, "iota8": iota8, "psel": psel, "tokv": tokv,
            "sgbase": sgbase, "buvfm": buvfm,
            "b1fm": np.ascontiguousarray(b1fm_all[c].T),
            "gq": gamma[0][:, None], "gk": gamma[1][:, None],
            "bq": beta[0][:, None], "bk": beta[1][:, None],
            "zero": zero,
        }
        maps.append(m)
    return maps


def kernel(**inputs):
    maps = _host_inputs(**inputs)
    nc = build(debug=False)
    res = run_bass_kernel_spmd(nc, maps, core_ids=list(range(NCORE)))
    return np.stack([res.results[c]["y"] for c in range(NCORE)], axis=0)


# version 22
# speedup vs baseline: 1.0285x; 1.0285x over previous
import numpy as np
import ml_dtypes
import concourse.bass as bass
import concourse.mybir as mybir
from concourse.bass_utils import run_bass_kernel_spmd
from concourse.bass_core_rust import IndirectOffsetOnAxis
from contextlib import ExitStack

F32 = mybir.dt.float32
BF16 = mybir.dt.bfloat16
U32 = mybir.dt.uint32
I32 = mybir.dt.int32
AF = mybir.ActivationFunctionType
ALU = mybir.AluOpType
AX = mybir.AxisListType.X

NCORE = 8
TOK = 2048
D = 1024
E2 = 2048
SDIM = 128
NEXP = 8
FF = 4096
NT = 16
NG = 4
CAP = 2048
BIGF = float(2 ** 25)
TABN = NEXP * CAP + 16512
EPS = 1e-5

V_TM = 0
QBF = 32768
KBF = 34816
WUV = 36864
XNF = 70656
UGRP = 74752
WOUTC = 36864
SQTC = 53248
UAVC = 61440
USBC = 69632
W1C = 0
W2C = 36864
SFMC = 32768
HBUFC = 69632
AB_COLS = 78848

XFO = 0
SCRO = 2048
PRTO = 3072
BASEO = 5120
LGO = 5632
M1O = 5760
RANKO = 5888
RT = 6016
SUMX, SUMX2, MUO, VVO, RSTDO, NBO = RT, RT + 16, RT + 32, RT + 48, RT + 64, RT + 80
G0O, LOCO, KEPTO, ESO, GFINO = RT + 96, RT + 112, RT + 128, RT + 144, RT + 160
DIDXO, SGMO, T1O, T2O, T3O = RT + 176, RT + 192, RT + 208, RT + 224, RT + 240
INVO, SIDXO, CIDXO, MXO = RT + 256, RT + 272, RT + 288, RT + 304
OFSO = RT + 320
AF_COLS = RT + 336
EXPO = BASEO
RAWVO = BASEO + 128
PFX1O = BASEO + 256
PFX2O = BASEO + 384


class _Fake:
    def __getattr__(self, name):
        return lambda *a, **k: None


class Marks:

    def __init__(self):
        self.record = True
        self.cnt = {}
        self.marks = {}
        self.sems = {}

    def inc(self, inst, sem, n, mark=None):
        if self.record:
            v = self.cnt.get(sem, 0) + n
            self.cnt[sem] = v
            if mark is not None:
                assert mark not in self.marks, mark
                self.marks[mark] = (sem, v)
        if inst is not None:
            inst.then_inc(self.sems[sem], n)

    def wait(self, eng, mark):
        s, v = self.marks[mark] if not self.record else (None, None)
        if self.record:
            assert mark in self.marks, f"forward mark reference: {mark}"
            return
        if not isinstance(eng, _Fake):
            eng.wait_ge(self.sems[s], v)

    def wait_sem(self, eng, sem, v):
        if self.record:
            return
        if not isinstance(eng, _Fake):
            eng.wait_ge(self.sems[sem], v)


def _emit(E, mk, C, debug):
    sy, gp, ac, ve, te = E["sy"], E["gp"], E["ac"], E["ve"], E["te"]
    AB, AFa = C["AB"], C["AFa"]

    def ab(off, n):
        return AB[:, off:off + n]

    def af(off, n):
        return AFa[:, off:off + n]

    PS = C["PS"]
    PT = C["PT"]

    if not isinstance(gp, _Fake):
        rbc = gp.alloc_register("rbc")
        gp.reg_mov(rbc, NEXP * CAP - 1)
        rb2 = gp.alloc_register("rb2")
        gp.reg_mov(rb2, TABN - 1)
    else:
        rbc = NEXP * CAP - 1
        rb2 = TABN - 1

    for name in ["ident", "sut", "onesc", "ones1f", "ones1b", "iota8", "psel",
                 "tokv", "sgbase", "buvfm", "b1fm", "gq", "gk", "bq", "bk"]:
        mk.inc(sy.dma_start(out=C[name + "_sb"][:], in_=C[name + "_d"]), "ds", 16)
    tz = C["tabl"].ap()[0:16384, 0:1].rearrange("(a b) c -> a (b c)", a=128)
    mk.inc(sy.dma_start(out=tz, in_=C["zero_d"]), "ds", 16, "c:tzero")

    wg3 = C["wg"].rearrange("(c p) n -> p c n", p=128)
    wgb3 = C["wgb_sb"][:].rearrange("p (c n) -> p c n", c=8)
    mk.inc(gp.dma_start(out=wgb3, in_=wg3), "dg", 16)
    mk.inc(gp.dma_start(out=C["buvv_sb"][:], in_=C["buv_row"][0:1, 2048:4096]), "dg", 16)
    mk.inc(gp.dma_start(out=C["boutr_sb"][:], in_=C["boutr"]), "dg", 16)
    mk.inc(gp.dma_start(out=C["b2r_sb"][:], in_=C["b2r"]), "dg", 16, "w:small")
    wuv3 = C["wuv"].rearrange("(c p) n -> p c n", p=128)
    wuv_sb3 = ab(WUV, 8 * 4224).rearrange("p (c n) -> p c n", c=8)
    mk.inc(gp.dma_start(out=wuv_sb3[:, :, 2048:4096], in_=wuv3[:, :, 2048:4096]), "dg", 16, "w:v0")
    mk.wait(gp, "w:v0")
    mk.inc(gp.dma_start(out=C["fence_b"].ap()[:, 0:512], in_=ab(WUV, 512)), "dg", 16, "w:v")
    mk.inc(gp.dma_start(out=wuv_sb3[:, :, 0:2048], in_=wuv3[:, :, 0:2048]), "dg", 16)
    mk.inc(gp.dma_start(out=wuv_sb3[:, :, 4096:4224], in_=wuv3[:, :, 4096:4224]), "dg", 16, "w:uv0")
    mk.wait(gp, "w:uv0")
    mk.inc(gp.dma_start(out=C["fence_b"].ap()[:, 0:512], in_=ab(WUV + 4224, 512)), "dg", 16, "w:uv")

    xnb = [C["xnb0"], C["xnb1"]]
    x_ap = C["x"]

    def p1_load(t):
        if t < 2:
            mk.wait(sy, f"pre:{14 + t}")
        if t >= 2:
            mk.wait(sy, f"xn:{t - 2}")
        mk.inc(sy.dma_start(out=af(XFO + (t % 2) * 1024, 1024),
                            in_=x_ap[t * 128:(t + 1) * 128, :]), "ds", 16, f"xf0:{t}")
        mk.wait(sy, f"xf0:{t}")
        mk.inc(sy.dma_start(out=C["fence_f_s"].ap()[:, 0:128],
                            in_=af(XFO + (t % 2) * 1024 + 896, 128)),
               "ds", 16, f"xf:{t}")

    def p1_pre(t):
        ld = sy if t % 2 == 0 else ac
        sm = "ds" if t % 2 == 0 else "da"
        if t >= 2:
            mk.wait(ld, f"pre:{t - 2}")
        mk.inc(ld.dma_start(out=af(XFO + (t % 2) * 1024, 1024),
                            in_=x_ap[t * 128:(t + 1) * 128, :]), sm, 16, f"pxf0:{t}")
        mk.wait(ld, f"pxf0:{t}")
        mk.inc(ld.dma_start(out=(C["fence_f_s"] if t % 2 == 0 else C["fence_f_a"]).ap()[:, 0:128],
                            in_=af(XFO + (t % 2) * 1024 + 896, 128)),
               sm, 16, f"pxf:{t}")
        mk.wait(ac, f"pxf:{t}")
        if t >= 1:
            mk.wait(ac, f"pre:{t - 1}")
        ac.activation(af(SCRO, 1024), af(XFO + (t % 2) * 1024, 1024), AF.Square)
        mk.inc(ac.drain(), "sa", 1, f"psq:{t}")
        mk.wait(ve, f"pxf:{t}")
        ve.tensor_reduce(af(SUMX + t, 1), af(XFO + (t % 2) * 1024, 1024), AX, ALU.add)
        mk.wait(ve, f"psq:{t}")
        mk.inc(ve.tensor_reduce(af(SUMX2 + t, 1), af(SCRO, 1024), AX, ALU.add),
               "sv", 1, f"pre:{t}")

    def p1_newton():
        n = 16
        ve.drain()
        ve.tensor_scalar(af(MUO, n), af(SUMX, n), 1.0 / D, None, ALU.mult)
        ve.tensor_scalar(af(VVO, n), af(SUMX2, n), 1.0 / D, EPS, ALU.mult, ALU.add)
        ve.drain()
        ve.tensor_mul(af(T1O, n), af(MUO, n), af(MUO, n))
        ve.drain()
        ve.tensor_sub(af(VVO, n), af(VVO, n), af(T1O, n))
        ve.drain()
        ve.tensor_scalar(af(VVO, n), af(VVO, n), 1e-6, None, ALU.max)
        ve.drain()
        ve.tensor_scalar(af(RSTDO, n), af(VVO, n), -0.5, 1.5, ALU.mult, ALU.add)
        for _ in range(3):
            ve.drain()
            ve.tensor_mul(af(T1O, n), af(RSTDO, n), af(RSTDO, n))
            ve.drain()
            ve.tensor_mul(af(T1O, n), af(VVO, n), af(T1O, n))
            ve.drain()
            ve.tensor_scalar(af(T1O, n), af(T1O, n), -0.5, 1.5, ALU.mult, ALU.add)
            ve.drain()
            ve.tensor_mul(af(RSTDO, n), af(RSTDO, n), af(T1O, n))
        ve.drain()
        ve.tensor_scalar(af(RSTDO, n), af(RSTDO, n), 0.0, 1e4, ALU.max, ALU.min)
        ve.drain()
        ve.tensor_mul(af(T2O, n), af(MUO, n), af(RSTDO, n))
        ve.drain()
        ve.tensor_scalar(af(NBO, n), af(T2O, n), -1.0, None, ALU.mult)
        mk.inc(ve.drain(), "sv", 1, "nball")

    def p1_xn(t):
        mk.wait(ac, "nball")
        mk.wait(ac, f"pxf:{t}") if False else None
        if t >= 1:
            mk.wait(ac, f"trall:{t - 1}")
            mk.wait(ac, f"xnpark:{t - 1}")
        mk.wait(ac, f"xf:{t}")
        mk.inc(ac.activation(xnb[0][:], af(XFO + (t % 2) * 1024, 1024), AF.Identity,
                             bias=af(NBO + t, 1), scale=af(RSTDO + t, 1)),
               "sa", 1, f"xn:{t}")

    def p1_tr(t):
        g, j = t // 4, t % 4
        mk.wait(te, f"xn:{t}")
        for jj in range(8):
            if t * 8 + jj >= 2:
                pj = t * 8 + jj - 2
                mk.wait(te, f"trev:{pj // 8}:{pj % 8}")
            inst = te.transpose(PT[jj % 2][:, 0:128],
                                xnb[0][:, jj * 128:(jj + 1) * 128],
                                C["ident_sb"][:])
            mk.inc(inst, "st", 1, f"tr:{t}:{jj}")
            mk.wait(ve, f"tr:{t}:{jj}")
            if j == 0 and jj == 0 and g > 0:
                mk.wait(ve, f"umm:{g - 1}:15")
            mk.inc(ve.tensor_copy(ab(XNF + jj * 512 + j * 128, 128),
                                  PT[jj % 2][:, 0:128]), "sv", 1, f"trev:{t}:{jj}")
        mk.inc(None, "st", 0)
        if mk.record:
            mk.marks[f"trall:{t}"] = ("st", mk.cnt["st"])

    def p1_body(t):
        g, j = t // 4, t % 4
        mk.wait(te, "w:small")
        mk.wait(te, f"trev:{t}:7")
        if t >= 1:
            mk.wait(te, f"lgev:{t - 1}")
        if j == 0 and t > 0:
            mk.wait(te, f"uev:{t // 4 - 1}:14")
        for f in range(8):
            inst = te.matmul(PS[6][:, 0:8], ab(XNF + f * 512 + j * 128, 128),
                             C["wgb_sb"][:, f * 8:(f + 1) * 8],
                             start=(f == 0), stop=(f == 7))
        mk.inc(inst, "st", 1, f"lgmm:{t}")
        mk.wait(ve, f"lgmm:{t}")
        mk.inc(ve.tensor_copy(af(LGO + t * 8, 8), PS[6][:, 0:8]), "sv", 1, f"lgev:{t}")
        mk.wait(te, "w:v")
        if t >= 1:
            mk.wait(te, f"vev:{t - 1}")
        for eb in range(4):
            te.matmul(PS[eb][:, 0:512], C["ones1b_sb"][:],
                      C["buvv_sb"][0:1, eb * 512:(eb + 1) * 512],
                      start=True, stop=False)
        for f in range(8):
            for eb in range(4):
                inst = te.matmul(PS[eb][:, 0:512],
                                 ab(XNF + f * 512 + j * 128, 128),
                                 ab(WUV + f * 4224 + 2048 + eb * 512, 512),
                                 start=False, stop=(f == 7))
        mk.inc(inst, "st", 1, f"vmm:{t}")
        if t + 1 < NT and (t + 1) % 4 != 0:
            p1_xn(t + 1)
            p1_tr(t + 1)
        mk.wait(ac, f"vmm:{t}")
        for eb in range(4):
            inst = ac.activation(ab(V_TM + t * 2048 + eb * 512, 512),
                                 PS[eb][:, 0:512], AF.Silu)
        mk.inc(inst, "sa", 1, f"vev:{t}")
        if t + 1 < NT and (t + 1) % 4 != 0:
            mk.wait(ac, f"vev:{t}")
            mk.inc(ac.dma_start(out=C["xnp_in"].ap()[(t + 1) * 128:(t + 2) * 128, :],
                                in_=xnb[0][:]), "da", 16, f"xnpark:{t + 1}")

    def p1_tail(g):
        mk.wait(te, f"lgev:{4 * g + 3}")
        for f in range(8):
            inst = te.matmul(PS[6][:, 0:512], ab(WUV + f * 4224 + 4096, 128),
                             ab(XNF + f * 512, 512), start=(f == 0), stop=(f == 7))
        mk.inc(inst, "st", 1, f"bsmm:{g}")
        mk.wait(ac, f"bsmm:{g}")
        if g > 0:
            mk.wait(ac, f"qk:{g - 1}")
        ac.activation(af(BASEO, 512), PS[6][:, 0:512], AF.Silu,
                      bias=C["buvfm_sb"][:, 32:33])
        mk.inc(ac.drain(), "sa", 1, f"bsev:{g}")
        mk.wait(ve, f"bsev:{g}")
        ve.tensor_scalar(ab(QBF + g * 512, 512), af(BASEO, 512),
                         C["gq_sb"][:, 0:1], C["bq_sb"][:, 0:1], ALU.mult, ALU.add)
        ve.tensor_scalar(ab(KBF + g * 512, 512), af(BASEO, 512),
                         C["gk_sb"][:, 0:1], C["bk_sb"][:, 0:1], ALU.mult, ALU.add)
        mk.inc(ve.drain(), "sv", 1, f"qk:{g}")
        for uu in range(16):
            pb = PS[6] if uu % 2 == 0 else PS[7]
            if uu == 0:
                mk.wait(te, f"bsev:{g}")
            if uu == 1 and g > 0:
                mk.wait(te, f"uev:{g - 1}:15")
            if uu >= 2:
                mk.wait(te, f"uev:{g}:{uu - 2}")
            for f in range(8):
                inst = te.matmul(pb[:, 0:512], ab(WUV + f * 4224 + uu * 128, 128),
                                 ab(XNF + f * 512, 512), start=(f == 0), stop=(f == 7))
            mk.inc(inst, "st", 1, f"umm:{g}:{uu}")
            mk.wait(ac, f"umm:{g}:{uu}")
            if uu % 8 == 0 and (g > 0 or uu == 8):
                ph = (g * 2 + uu // 8 - 1)
                mk.wait(ac, f"upark:{ph // 2}:{ph % 2}")
            inst = ac.activation(ab(UGRP + (uu % 8) * 512, 512), pb[:, 0:512],
                                 AF.Silu, bias=C["buvfm_sb"][:, uu:uu + 1])
            mk.inc(inst, "sa", 1, f"uev:{g}:{uu}")
            if uu % 8 == 7:
                h = uu // 8
                mk.inc(ac.drain(), "sa", 1, f"uevd:{g}:{h}")
                mk.wait(ac, f"uevd:{g}:{h}")
                mk.inc(ac.dma_start(out=C["upark"].ap()[g, :, h * 4096:(h + 1) * 4096],
                                    in_=ab(UGRP, 4096)), "da", 16, f"upark:{g}:{h}")

    for _t in range(NT):
        p1_pre(_t)
    mk.wait(gp, "pre:15")
    wout3p = C["wout"].rearrange("(c p) n -> p c n", p=128)
    woutbf3 = C["wout_bf"].ap().rearrange("p (c n) -> p c n", c=16)
    mk.inc(gp.dma_start(out=woutbf3, in_=wout3p), "dg", 16, "c:woutd")
    w13p = C["w1"].rearrange("(c p) n -> p c n", p=128)
    w1bf3 = C["w1_bf"].ap().rearrange("p (c n) -> p c n", c=8)
    mk.inc(gp.dma_start(out=w1bf3, in_=w13p), "dg", 16, "c:w1d")
    p1_newton()
    p1_load(0)
    p1_load(1)
    p1_xn(0)
    mk.inc(ac.drain(), "sa", 1, "xnd:0")
    p1_tr(0)
    mk.wait(ac, "xnd:0")
    mk.inc(ac.dma_start(out=C["xnp_in"].ap()[0:128, :], in_=xnb[0][:]),
           "da", 16, "xnpark:0")
    for t in range(NT):
        p1_body(t)
        if t + 2 < NT:
            p1_load(t + 2)
        if t % 4 == 3:
            p1_tail(t // 4)
            if t + 1 < NT:
                p1_xn(t + 1)
                mk.inc(ac.drain(), "sa", 1, f"xnd:{t + 1}")
                p1_tr(t + 1)
                mk.wait(ac, f"xnd:{t + 1}")
                mk.inc(ac.dma_start(
                    out=C["xnp_in"].ap()[(t + 1) * 128:(t + 2) * 128, :],
                    in_=xnb[0][:]), "da", 16, f"xnpark:{t + 1}")
            k1 = t // 4
            mk.wait(gp, f"xnpark:{t}")
            mk.inc(gp.collective_compute(
                "AllGather", ALU.bypass,
                replica_groups=[list(range(NCORE))],
                ins=[C["xnp_in"].ap()[(t - 3) * 128:(t + 1) * 128, :]],
                outs=[C["xnp"].ap()[k1 * 4096:(k1 + 1) * 4096, :]]),
                "cc1", 1, f"ag1:{k1}")
    mk.wait(ve, "qk:3")
    ve.tensor_scalar(af(RAWVO, 128), af(LGO, 128), 60.0, None, ALU.min)
    mk.inc(ve.drain(), "sv", 1, "lgclamp")
    mk.wait(ac, "lgclamp")
    ac.activation(af(EXPO, 128), af(RAWVO, 128), AF.Exp)
    mk.inc(ac.drain(), "sa", 1, "expall")
    mk.wait(ve, "expall")
    for t in range(NT):
        ve.tensor_reduce(af(T2O + t, 1), af(EXPO + t * 8, 8), AX, ALU.add)
        ve.tensor_reduce(af(MXO + t, 1), af(EXPO + t * 8, 8), AX, ALU.max)
    ve.drain()
    ve.reciprocal(af(T3O, 16), af(T2O, 16))
    ve.drain()
    ve.tensor_mul(af(G0O, 16), af(MXO, 16), af(T3O, 16))
    mk.inc(ve.drain(), "sv", 1, "g0all")
    for t in range(NT):
        ve.tensor_scalar(af(RAWVO + t * 8, 8), af(EXPO + t * 8, 8),
                         af(MXO + t, 1), None, ALU.is_ge)
    ve.drain()
    r3 = af(RAWVO, 128).rearrange("p (t e) -> p t e", e=8)
    p13 = af(PFX1O, 128).rearrange("p (t e) -> p t e", e=8)
    p23 = af(PFX2O, 128).rearrange("p (t e) -> p t e", e=8)
    ve.tensor_copy(p13[:, :, 0:1], r3[:, :, 0:1])
    ve.tensor_add(p13[:, :, 1:8], r3[:, :, 1:8], r3[:, :, 0:7])
    ve.drain()
    ve.tensor_copy(p23[:, :, 0:2], p13[:, :, 0:2])
    ve.tensor_add(p23[:, :, 2:8], p13[:, :, 2:8], p13[:, :, 0:6])
    ve.drain()
    ve.tensor_copy(p13[:, :, 0:4], p23[:, :, 0:4])
    ve.tensor_add(p13[:, :, 4:8], p23[:, :, 4:8], p23[:, :, 0:4])
    ve.drain()
    ve.tensor_sub(af(PFX2O, 128), af(PFX1O, 128), af(RAWVO, 128))
    ve.drain()
    ve.tensor_scalar(af(PFX2O, 128), af(PFX2O, 128), 0.5, None, ALU.is_lt)
    ve.drain()
    ve.tensor_mul(af(M1O, 128), af(RAWVO, 128), af(PFX2O, 128))
    mk.inc(ve.drain(), "sv", 1, "m1all")
    for t in range(NT):
        mk.wait(te, "m1all")
        if t == 0:
            mk.wait(te, "vev:15")
        inst = te.matmul(PS[1][0:1, t * 8:(t + 1) * 8], C["onesc_sb"][:],
                         af(M1O + t * 8, 8), start=True, stop=True)
    mk.inc(inst, "st", 1, "csall")
    mk.wait(ve, "csall")
    ve.tensor_copy(C["csrow_sb"][:], PS[1][0:1, 0:128])
    mk.inc(ve.drain(), "sv", 1, "csrowev")
    cs3 = C["csrow_sb"][:].rearrange("p (t e) -> p t e", e=8)
    a3 = C["rowa_sb"][:].rearrange("p (t e) -> p t e", e=8)
    b3 = C["rowb_sb"][:].rearrange("p (t e) -> p t e", e=8)
    ve.tensor_copy(a3[:, 0:1, :], cs3[:, 0:1, :])
    ve.tensor_add(a3[:, 1:16, :], cs3[:, 1:16, :], cs3[:, 0:15, :])
    ve.drain()
    ve.tensor_copy(b3[:, 0:2, :], a3[:, 0:2, :])
    ve.tensor_add(b3[:, 2:16, :], a3[:, 2:16, :], a3[:, 0:14, :])
    ve.drain()
    ve.tensor_copy(a3[:, 0:4, :], b3[:, 0:4, :])
    ve.tensor_add(a3[:, 4:16, :], b3[:, 4:16, :], b3[:, 0:12, :])
    ve.drain()
    ve.tensor_copy(b3[:, 0:8, :], a3[:, 0:8, :])
    ve.tensor_add(b3[:, 8:16, :], a3[:, 8:16, :], a3[:, 0:8, :])
    ve.drain()
    ve.tensor_sub(C["rowa_sb"][:], C["rowb_sb"][:], C["csrow_sb"][:])
    ve.tensor_copy(C["hist_row_sb"][:], C["rowb_sb"][0:1, 120:128])
    mk.inc(ve.drain(), "sv", 1, "pfxd")
    mk.wait(ac, "pfxd")
    mk.inc(ac.dma_start(out=C["hist_in"].ap(), in_=C["hist_row_sb"][:]),
           "da", 16, "histw")
    for t in range(NT):
        pb = PS[t % 2]
        if t < 2:
            mk.wait(te, "pfxd")
            if t == 1:
                mk.wait(te, "csrowev")
        else:
            mk.wait(te, f"rkev:{t - 2}")
        te.matmul(pb[:, 0:8], C["sut_sb"][:], af(M1O + t * 8, 8),
                  start=True, stop=False)
        mk.inc(te.matmul(pb[:, 0:8], C["ones1f_sb"][:], C["rowa_sb"][0:1, t * 8:(t + 1) * 8],
                         start=False, stop=True), "st", 1, f"rkmm:{t}")
        mk.wait(ve, f"rkmm:{t}")
        mk.inc(ve.tensor_copy(af(RANKO + t * 8, 8), pb[:, 0:8]), "sv", 1, f"rkev:{t}")
    mk.inc(ve.drain(), "sv", 1, "rkevall")

    mk.wait(sy, "umm:3:15")
    mk.wait(sy, "c:woutd")
    mk.inc(sy.dma_start(out=ab(WOUTC, 16 * 1024), in_=C["wout_bf"].ap()),
           "ds", 16, "woutld0")
    mk.wait(sy, "woutld0")
    mk.inc(sy.dma_start(out=C["fence_b_s"].ap()[:, 0:512], in_=ab(WOUTC, 512)),
           "ds", 16, "woutld")
    for qb in range(NG):
        for uh in range(2):
            if qb > 0:
                mk.wait(gp, f"uavmul:{qb - 1}:{8 * uh + 7}")
            if qb == 0 and uh == 0:
                mk.wait(gp, "upark:3:1")
            mk.inc(gp.dma_start(out=ab(USBC + uh * 4096, 4096),
                                in_=C["upark"].ap()[qb, :, uh * 4096:(uh + 1) * 4096]),
                   "dg", 16, f"uld:{qb}:{uh}")
        if qb == 0:
            mk.wait(gp, "histw")
            mk.inc(gp.collective_compute("AllGather", ALU.bypass,
                                         replica_groups=[list(range(NCORE))],
                                         ins=[C["hist_in"].ap()], outs=[C["hist_out"].ap()]),
                   "cc2", 1, "aghist")
        if qb == 1:
            mk.wait_sem(gp, "cc2", 1)
            mk.inc(gp.dma_start(out=C["hist_sb"][:], in_=C["hist_out"].ap()),
                   "dg", 16, "histr")
        if qb == 2:
            mk.wait(gp, "idxu")
            mk.wait(gp, "c:tzero")
            for t in range(NT):
                mk.inc(gp.indirect_dma_start(
                    out=C["tabl"].ap(),
                    out_offset=IndirectOffsetOnAxis(C["didx_sb"][:, t:t + 1], 0),
                    in_=C["tokv_sb"][:, t:t + 1], in_offset=None,
                    bounds_check=rb2, oob_is_err=False), "dg", 16, f"tsc:{t}")
            mk.wait(gp, "tsc:15")
            mk.inc(gp.dma_start(out=C["fence_f"].ap(),
                                in_=C["tabl"].ap()[0:16384, 0:1].rearrange("(a b) c -> a (b c)", a=128)),
                   "dg", 16, "tscf")
            mk.wait(gp, "tscf")
            mk.inc(gp.collective_compute("AllReduce", ALU.add,
                                         replica_groups=[list(range(NCORE))],
                                         ins=[C["tabl"].ap()], outs=[C["tabl_red"].ap()]),
                   "cc3", 1, "artab")
        for mt in range(NT):
            pb = PS[mt % 2]
            if qb == 0 and mt < 2:
                mk.wait(te, "qk:3")
                mk.wait(te, "rkevall")
            if qb > 0 and mt < 2:
                mk.wait(te, f"prtadd:{4 * qb - 1}")
            if qb * NT + mt >= 2:
                p = qb * NT + mt - 2
                mk.wait(te, f"srel:{p // NT}:{p % NT}")
            mk.inc(te.matmul(pb[:, 0:512], ab(KBF + mt * 128, 128),
                             ab(QBF + qb * 512, 512), start=True, stop=True),
                   "st", 1, f"smm:{qb}:{mt}")
            mk.wait(ve, f"smm:{qb}:{mt}")
            scr = af(SCRO + (mt % 2) * 512, 512)
            mk.inc(ve.tensor_scalar(scr, pb[:, 0:512], 0.0, None, ALU.max),
                   "sv", 1, f"srel:{qb}:{mt}")
            mk.wait(ac, f"srel:{qb}:{mt}")
            mk.inc(ac.activation(ab(SQTC + mt * 512, 512), scr, AF.Square,
                                 scale=1.0 / TOK), "sa", 1, f"sqev:{qb}:{mt}")
        for et in range(NT):
            pb = PS[2 + et % 2]
            if qb == 2 and et < 2:
                mk.wait(te, "ofsbev")
            if qb * NT + et >= 2:
                p = qb * NT + et - 2
                mk.wait(te, f"uavmul:{p // NT}:{p % NT}")
            mk.wait(te, f"sqev:{qb}:15")
            for mt in range(NT):
                inst = te.matmul(pb[:, 0:512],
                                 ab(V_TM + mt * 2048 + et * 128, 128),
                                 ab(SQTC + mt * 512, 512),
                                 start=(mt == 0), stop=(mt == NT - 1))
            mk.inc(inst, "st", 1, f"avmm:{qb}:{et}")
            mk.wait(ve, f"avmm:{qb}:{et}")
            mk.wait(ve, f"uld:{qb}:{et // 8}")
            mk.inc(ve.tensor_mul(ab(UAVC + et * 512, 512), pb[:, 0:512],
                                 ab(USBC + et * 512, 512)), "sv", 1, f"uavmul:{qb}:{et}")
        mk.inc(None, "sv", 0)
        if mk.record:
            mk.marks[f"uavall:{qb}"] = ("sv", mk.cnt["sv"])
        for s2 in range(4):
            t4 = qb * 4 + s2
            pr = (PS[6], PS[7]) if s2 % 2 == 0 else (PS[0], PS[1])
            mk.wait(te, f"uavmul:{qb}:15")
            if qb == 0 and s2 == 0:
                mk.wait(te, "woutld")
                mk.wait(te, "uev:3:15")
            if qb * 4 + s2 >= 2:
                p = qb * 4 + s2 - 2
                mk.wait(te, f"prtadd:{p}")
            te.matmul(pr[0][:, 0:512], C["ones1b_sb"][:], C["boutr_sb"][0:1, 0:512],
                      start=True, stop=False)
            te.matmul(pr[1][:, 0:512], C["ones1b_sb"][:], C["boutr_sb"][0:1, 512:1024],
                      start=True, stop=False)
            for et in range(NT):
                lhs = ab(UAVC + et * 512 + s2 * 128, 128)
                te.matmul(pr[0][:, 0:512], lhs, ab(WOUTC + et * 1024, 512),
                          start=False, stop=(et == NT - 1))
                inst = te.matmul(pr[1][:, 0:512], lhs, ab(WOUTC + et * 1024 + 512, 512),
                                 start=False, stop=(et == NT - 1))
            mk.inc(inst, "st", 1, f"omm:{qb}:{s2}")
            if t4 < 2:
                mk.wait(sy, f"xn:{14 + t4}")
            else:
                mk.wait(sy, f"prtadd:{t4 - 2}")
            mk.inc(sy.dma_start(out=af(XFO + (t4 % 2) * 1024, 1024),
                                in_=x_ap[t4 * 128:(t4 + 1) * 128, :]),
                   "ds", 16, f"xf4:{t4}")
            mk.wait(ve, f"omm:{qb}:{s2}")
            mk.wait(ve, f"xf4:{t4}")
            if t4 >= 2:
                mk.wait(ve, f"prtw:{t4 - 2}")
            xf4 = af(XFO + (t4 % 2) * 1024, 1024)
            prt = af(PRTO + (t4 % 2) * 1024, 1024)
            ve.tensor_add(prt[:, 0:512], pr[0][:, 0:512], xf4[:, 0:512])
            ve.tensor_add(prt[:, 512:1024], pr[1][:, 0:512], xf4[:, 512:1024])
            mk.inc(ve.drain(), "sv", 1, f"prtadd:{t4}")
            mk.wait(sy, f"prtadd:{t4}")
            mk.inc(sy.dma_start(out=C["partial"].ap()[t4 * 128:(t4 + 1) * 128, :],
                                in_=prt), "ds", 16, f"prtw:{t4}")
        if qb == 1 and s2 == 3:
            mk.wait(te, "histr")
            mk.wait(te, "uavmul:1:15")
            mk.inc(te.matmul(PS[2][0:1, 0:8], C["psel_sb"][:], C["hist_sb"][:],
                             start=True, stop=True), "st", 1, "offmm")
            mk.wait(ve, "offmm")
            ve.tensor_copy(C["offsr_sb"][:], PS[2][0:1, 0:8])
            mk.inc(ve.drain(), "sv", 1, "offev")
            mk.wait(te, "offev")
            mk.inc(te.matmul(PS[3][0:128, 0:8], C["ones1f_sb"][:], C["offsr_sb"][:],
                             start=True, stop=True), "st", 1, "ofbmm")
            mk.wait(ve, "ofbmm")
            ve.tensor_copy(af(OFSO, 8), PS[3][:, 0:8])
            mk.inc(ve.drain(), "sv", 1, "ofsbev")
            ve.drain()
            for t in range(NT):
                ve.tensor_add(af(RAWVO + t * 8, 8), af(RANKO + t * 8, 8), af(OFSO, 8))
            ve.drain()
            ve.tensor_mul(af(RAWVO, 128), af(RAWVO, 128), af(M1O, 128))
            ve.tensor_mul(af(PFX1O, 128), af(M1O, 128), C["iota8_sb"][:])
            ve.drain()
            for t in range(NT):
                ve.tensor_reduce(af(LOCO + t, 1), af(RAWVO + t * 8, 8), AX, ALU.add)
                ve.tensor_reduce(af(ESO + t, 1), af(PFX1O + t * 8, 8), AX, ALU.add)
            ve.drain()
            ve.tensor_scalar(af(KEPTO, 16), af(LOCO, 16), float(CAP), None, ALU.is_lt)
            ve.drain()
            ve.tensor_mul(af(GFINO, 16), af(G0O, 16), af(KEPTO, 16))
            mk.inc(ve.drain(), "sv", 1, "gfin")
            mk.wait(ac, "gfin")
            ac.copy(C["loci_sb"][:], af(LOCO, 16))
            mk.inc(ac.drain(), "sa", 1, "loci")
            mk.wait(ve, "loci")
            ve.tensor_scalar(C["hi_sb"][:], C["loci_sb"][:], 7, None, ALU.arith_shift_right)
            ve.tensor_scalar(C["lo_sb"][:], C["loci_sb"][:], 127, None, ALU.bitwise_and)
            ve.drain()
            ve.scalar_tensor_tensor(af(SGMO, 16), C["lo_sb"][:], 16.0, C["hi_sb"][:],
                                    ALU.mult, ALU.add)
            ve.tensor_scalar(af(PFX2O, 16), af(LOCO, 16), 1536.0, None, ALU.is_ge)
            ve.scalar_tensor_tensor(af(RAWVO, 16), af(ESO, 16), 1536.0, af(LOCO, 16),
                                    ALU.mult, ALU.add)
            ve.scalar_tensor_tensor(af(INVO, 16), af(ESO, 16), 512.0, af(LOCO, 16),
                                    ALU.mult, ALU.add)
            ve.drain()
            ve.tensor_scalar(af(INVO, 16), af(INVO, 16), 1.0, 10752.0, ALU.mult, ALU.add)
            ve.drain()
            ve.tensor_sub(af(INVO, 16), af(INVO, 16), af(RAWVO, 16))
            ve.drain()
            ve.tensor_mul(af(INVO, 16), af(INVO, 16), af(PFX2O, 16))
            ve.drain()
            ve.tensor_add(af(MXO, 16), af(RAWVO, 16), af(INVO, 16))
            ve.tensor_scalar(af(T1O, 16), af(ESO, 16), float(CAP), None, ALU.mult)
            ve.tensor_scalar(af(T3O, 16), af(KEPTO, 16), -1.0, 1.0, ALU.mult, ALU.add)
            ve.drain()
            ve.tensor_add(af(T2O, 16), af(T1O, 16), af(SGMO, 16))
            ve.tensor_copy(af(T1O, 16), af(MXO, 16))
            ve.tensor_mul(af(PFX1O, 16), af(T3O, 16), C["tokv_sb"][:])
            ve.tensor_scalar(af(PFX2O, 16), af(T3O, 16), BIGF, None, ALU.mult)
            ve.tensor_scalar(af(PFX2O + 16, 16), af(T3O, 16), float(NEXP * CAP), None, ALU.mult)
            ve.drain()
            ve.tensor_mul(af(T2O, 16), af(T2O, 16), af(KEPTO, 16))
            ve.tensor_mul(af(T1O, 16), af(T1O, 16), af(KEPTO, 16))
            ve.tensor_add(af(PFX2O + 16, 16), af(PFX1O, 16), af(PFX2O + 16, 16))
            ve.tensor_add(af(PFX1O, 16), af(PFX1O, 16), af(PFX2O, 16))
            ve.drain()
            ve.tensor_add(af(DIDXO, 16), af(T2O, 16), af(PFX2O + 16, 16))
            ve.tensor_add(af(CIDXO, 16), af(T1O, 16), af(PFX1O, 16))
            ve.drain()
            ve.tensor_scalar(af(T3O, 16), af(CIDXO, 16), 12288.0, None, ALU.is_lt)
            ve.drain()
            ve.scalar_tensor_tensor(af(MXO, 16), af(T3O, 16), BIGF, af(CIDXO, 16),
                                    ALU.mult, ALU.add)
            mk.inc(ve.drain(), "sv", 1, "cidxf")
            mk.wait(ac, "cidxf")
            ac.copy(C["didx_sb"][:], af(DIDXO, 16))
            ac.copy(C["cidx_sb"][:], af(CIDXO, 16))
            ac.copy(C["cidx2_sb"][:], af(MXO, 16))
            mk.inc(ac.drain(), "sa", 1, "idxu")


    mk.wait_sem(gp, "cc3", 1)
    mk.inc(gp.indirect_dma_start(
        out=af(INVO, 16), out_offset=None,
        in_=C["tabl_red"].ap(),
        in_offset=IndirectOffsetOnAxis(C["sgbase_sb"][:, 0:1], 0)),
        "dg", 16, "invg0")
    mk.wait(gp, "invg0")
    mk.inc(gp.dma_start(out=C["fence_f"].ap()[:, 0:16], in_=af(INVO, 16)), "dg", 16, "invg")
    mk.wait(ve, "invg")
    ve.tensor_scalar(af(T1O, 16), af(INVO, 16), 0.5, None, ALU.is_lt)
    ve.drain()
    ve.scalar_tensor_tensor(af(T2O, 16), af(T1O, 16), BIGF, af(INVO, 16),
                            ALU.mult, ALU.add)
    ve.drain()
    ve.tensor_scalar(af(SIDXO, 16), af(T2O, 16), 1.0, -1.0, ALU.mult, ALU.add)
    mk.inc(ve.drain(), "sv", 1, "sidxf")
    mk.wait(ac, "sidxf")
    ac.copy(C["loci_sb"][:], af(SIDXO, 16))
    mk.inc(ac.drain(), "sa", 1, "sidxi")
    mk.wait(ve, "sidxi")
    ve.tensor_scalar(C["hi_sb"][:], C["loci_sb"][:], 9, None, ALU.arith_shift_right)
    ve.tensor_scalar(C["k9_sb"][:], C["loci_sb"][:], 11, None, ALU.arith_shift_right)
    ve.tensor_scalar(C["l9_sb"][:], C["loci_sb"][:], 511, None, ALU.bitwise_and)
    ve.drain()
    ve.tensor_scalar(C["lo_sb"][:], C["hi_sb"][:], 3, None, ALU.bitwise_and)
    ve.drain()
    ve.scalar_tensor_tensor(af(T1O, 16), C["k9_sb"][:], 512.0, C["l9_sb"][:],
                            ALU.mult, ALU.add)
    ve.drain()
    ve.scalar_tensor_tensor(af(SIDXO, 16), C["lo_sb"][:], 4096.0, af(T1O, 16),
                            ALU.mult, ALU.add)
    mk.inc(ve.drain(), "sv", 1, "sidxf2")
    mk.wait(ac, "sidxf2")
    ac.copy(C["sidx_sb"][:], af(SIDXO, 16))
    mk.inc(ac.drain(), "sa", 1, "sidxu")

    ve.memset(C["slots0"][:], 0.0)
    ve.memset(C["slots1"][:], 0.0)
    mk.inc(ve.drain(), "sv", 1, "slz")
    slots = [C["slots0"], C["slots1"]]
    eo_pairs = [(PS[0], PS[1]), (PS[6], PS[7])]
    last_pair_mark = [None, None]
    for b in range(4):
        for sp in range(2):
            for st in (2 * sp, 2 * sp + 1):
                gi = b * 4 + st
                mk.wait(gp, "sidxu")
                mk.wait_sem(gp, "cc1", 4)
                if gi == 0:
                    mk.wait(gp, "slz")
                if gi >= 2:
                    p = gi - 2
                    mk.wait(gp, f"stev:{p // 4}:{p % 4}:7")
                mk.inc(gp.indirect_dma_start(
                    out=slots[st % 2][:], out_offset=None,
                    in_=C["xnp"].ap(),
                    in_offset=IndirectOffsetOnAxis(C["sidx_sb"][:, gi:gi + 1], 0),
                    bounds_check=rbc, oob_is_err=False),
                    "dg", 16, f"sg:{b}:{st}")
            st1 = 2 * sp + 1
            mk.wait(gp, f"sg:{b}:{st1}")
            mk.inc(gp.dma_start(
                out=(C["d_slots"].ap() if (debug and b == 0 and sp == 0) else
                     C["fence_b"].ap()),
                in_=slots[1][:]), "dg", 16, f"sgf:{b}:{st1}")
            for st in (2 * sp, 2 * sp + 1):
                gi = b * 4 + st
                mk.wait(te, f"sgf:{b}:{st1}")
                for jj in range(8):
                    k = gi * 8 + jj
                    if k >= 2:
                        pk = k - 2
                        mk.wait(te, f"stev:{pk // 32}:{(pk % 32) // 8}:{pk % 8}")
                    inst = te.transpose(PT[jj % 2][:, 0:128],
                                        slots[st % 2][:, jj * 128:(jj + 1) * 128],
                                        C["ident_sb"][:])
                    mk.inc(inst, "st", 1, f"sttrj:{b}:{st}:{jj}")
                    mk.wait(ve, f"sttrj:{b}:{st}:{jj}")
                    mk.inc(ve.tensor_copy(ab(SFMC + jj * 512 + st * 128, 128),
                                          PT[jj % 2][:, 0:128]),
                           "sv", 1, f"stev:{b}:{st}:{jj}")
                mk.inc(None, "st", 0)
                if mk.record:
                    mk.marks[f"sttr:{b}:{st}"] = ("st", mk.cnt["st"])
        if b == 0:
            mk.wait(ac, "avmm:3:15")
            mk.wait(ac, "c:w1d")
            mk.inc(ac.dma_start(out=ab(W1C, 8 * 4096), in_=C["w1_bf"].ap()),
                   "da", 16, "w1ld0")
            mk.wait(ac, "w1ld0")
            mk.inc(ac.dma_start(out=C["fence_b_a"].ap()[:, 0:512], in_=ab(W1C, 512)),
                   "da", 16, "w1ld:3")
            mk.wait(gp, "omm:3:3")
            w23 = C["w2"].rearrange("(c p) n -> p c n", p=128)
            w2sb3 = ab(W2C, 32 * 1024).rearrange("p (c n) -> p c n", c=32)
            for wc in range(2):
                mk.inc(gp.dma_start(out=w2sb3[:, 16 * wc:16 * wc + 16, :],
                                    in_=w23[:, 16 * wc:16 * wc + 16, :]), "dg", 16)
                mk.inc(gp.dma_start(out=C["fence_b"].ap()[:, 0:512],
                                    in_=ab(W2C + 16 * wc * 1024, 512)),
                       "dg", 16, f"w2ld:{wc}")
        for h2 in range(2):
            for ft in range(32):
                pb = PS[2 + ft % 2]
                if b == 0 and h2 == 0 and ft < 2:
                    mk.wait(te, "w1ld:3")
                    mk.wait(te, "uavmul:3:15")
                if h2 == 0 and ft == 0:
                    mk.wait(te, f"stev:{b}:3:7")
                k = (b * 2 + h2) * 32 + ft
                if k >= 2:
                    pk = k - 2
                    mk.wait(te, f"hev:{pk // 64}:{(pk % 64) // 32}:{pk % 32}")
                for m in range(8):
                    inst = te.matmul(pb[:, 0:256],
                                     ab(W1C + m * 4096 + ft * 128, 128),
                                     ab(SFMC + m * 512 + h2 * 256, 256),
                                     start=(m == 0), stop=(m == 7))
                mk.inc(inst, "st", 1, f"hmm:{b}:{h2}:{ft}")
                mk.wait(ac, f"hmm:{b}:{h2}:{ft}")
                if b == 0 and h2 == 0 and ft == 0:
                    mk.wait(ac, "upark:3:1")
                inst = ac.activation(ab(HBUFC + ft * 256, 256), pb[:, 0:256],
                                     AF.Gelu_apprx_tanh, bias=C["b1fm_sb"][:, ft:ft + 1])
                mk.inc(inst, "sa", 1, f"hev:{b}:{h2}:{ft}")
            for sub in range(2):
                pi = sub
                pr = eo_pairs[pi]
                srow = b * 512 + h2 * 256 + sub * 128
                mk.wait(te, f"hev:{b}:{h2}:31")
                if last_pair_mark[pi] is not None:
                    mk.wait(te, last_pair_mark[pi])
                elif pi == 0:
                    mk.wait(te, "prtadd:15")
                else:
                    mk.wait(te, "prtadd:14")
                mk.wait(te, "w2ld:1" if sub or True else "w2ld:0")
                te.matmul(pr[0][:, 0:512], C["ones1b_sb"][:], C["b2r_sb"][0:1, 0:512],
                          start=True, stop=False)
                te.matmul(pr[1][:, 0:512], C["ones1b_sb"][:], C["b2r_sb"][0:1, 512:1024],
                          start=True, stop=False)
                for ft in range(32):
                    lhs = ab(HBUFC + ft * 256 + sub * 128, 128)
                    te.matmul(pr[0][:, 0:512], lhs, ab(W2C + ft * 1024, 512),
                              start=False, stop=(ft == 31))
                    inst = te.matmul(pr[1][:, 0:512], lhs, ab(W2C + ft * 1024 + 512, 512),
                                     start=False, stop=(ft == 31))
                mk.inc(inst, "st", 1, f"eomm:{b}:{h2}:{sub}")
                eb = C["eob0"] if (srow // 128) % 2 == 0 else C["eob1"]
                mk.wait(ve, f"eomm:{b}:{h2}:{sub}")
                if srow // 128 >= 2:
                    mk.wait(ve, f"eow:{srow // 128 - 2}")
                ve.tensor_copy(eb[:, 0:512], pr[0][:, 0:512])
                ve.tensor_copy(eb[:, 512:1024], pr[1][:, 0:512])
                mk.inc(ve.drain(), "sv", 1, f"eoev:{b}:{h2}:{sub}")
                last_pair_mark[pi] = f"eoev:{b}:{h2}:{sub}"
                mk.wait(sy, f"eoev:{b}:{h2}:{sub}")
                mk.inc(sy.dma_start(out=C["eo_in"].ap()[srow:srow + 128, :], in_=eb[:]),
                       "ds", 16, f"eow:{srow // 128}")
                if srow in (1408, 1920):
                    kch = 0 if srow == 1408 else 1
                    lo5, hi5 = (0, 1536) if kch == 0 else (1536, 2048)
                    plo, phi = (0, 12288) if kch == 0 else (12288, 16384)
                    if kch == 1:
                        mk.wait_sem(gp, "c40", 1)
                        mk.wait(gp, "hmm:3:1:31")
                        for t5 in range(NT):
                            mk.inc(gp.indirect_dma_start(
                                out=ab(W1C + t5 * 1024, 1024), out_offset=None,
                                in_=C["eo_pool"].ap(),
                                in_offset=IndirectOffsetOnAxis(
                                    C["cidx_sb"][:, t5:t5 + 1], 0),
                                bounds_check=rbc, oob_is_err=False),
                                "dg", 16, f"fg0:{t5}")
                    mk.wait(gp, f"eow:{hi5 // 128 - 1}")
                    mk.inc(gp.collective_compute(
                        "AllGather", ALU.bypass,
                        replica_groups=[list(range(NCORE))],
                        ins=[C["eo_in"].ap()[lo5:hi5, :]],
                        outs=[C["eo_pool"].ap()[plo:phi, :]]),
                        f"c4{kch}", 1, f"ag3:{kch}")


    for tp_ in range(8):
        for t in (2 * tp_, 2 * tp_ + 1):
            if t == 0:
                mk.wait_sem(gp, "c41", 1)
            mk.inc(gp.indirect_dma_start(
                out=ab(W1C + t * 1024, 1024), out_offset=None,
                in_=C["eo_pool"].ap(),
                in_offset=IndirectOffsetOnAxis(C["cidx2_sb"][:, t:t + 1], 0),
                bounds_check=rbc, oob_is_err=False),
                "dg", 16, f"fg1:{t}")
        t1 = 2 * tp_ + 1
        mk.wait(gp, f"fg1:{t1}")
        mk.inc(gp.dma_start(out=C["fence_b"].ap(), in_=ab(W1C + t1 * 1024, 1024)),
               "dg", 16, f"fg:{t1}")
        for t in (2 * tp_, 2 * tp_ + 1):
            if t == 0:
                mk.wait(ac, "prtw:15")
            if t >= 2:
                mk.wait(ac, f"fmul:{t - 2}")
            prt = af(PRTO + (t % 2) * 1024, 1024)
            mk.inc(ac.dma_start(out=prt, in_=C["partial"].ap()[t * 128:(t + 1) * 128, :]),
                   "da", 16, f"pld:{t}")
            mk.wait(ve, f"fg:{t1}")
            mk.wait(ve, f"pld:{t}")
            if t >= 2:
                mk.wait(ve, f"yw:{t - 2}")
            outb = af(XFO + (t % 2) * 1024, 1024)
            ve.scalar_tensor_tensor(outb, ab(W1C + t * 1024, 1024), af(GFINO + t, 1),
                                    prt, ALU.mult, ALU.add)
            mk.inc(ve.drain(), "sv", 1, f"fmul:{t}")
            mk.wait(sy, f"fmul:{t}")
            mk.inc(sy.dma_start(out=C["y"].ap()[t * 128:(t + 1) * 128, :], in_=outb),
                   "ds", 16, f"yw:{t}")
    if debug:
        taps = [
            ("d_mu", af(MUO, 16), "gfin"),
            ("d_vv", af(VVO, 16), "gfin"),
            ("d_rstd", af(RSTDO, 16), "gfin"),
            ("d_sumx2", af(SUMX2, 16), "gfin"),
            ("d_xf", af(XFO, 2048), "yw:15"),
            ("d_lg", af(LGO, 128), "gfin"),
            ("d_m1", af(M1O, 128), "gfin"),
            ("d_rank", af(RANKO, 128), "gfin"),
            ("d_g0", af(G0O, 16), "gfin"),
            ("d_loc", af(LOCO, 16), "gfin"),
            ("d_kept", af(KEPTO, 16), "gfin"),
            ("d_es", af(ESO, 16), "cidxf"),
            ("d_gfin", af(GFINO, 16), "gfin"),
            ("d_didx", af(DIDXO, 16), "cidxf"),
            ("d_cidx", af(CIDXO, 16), "cidxf"),
            ("d_inv", af(INVO, 16), "sidxf"),
            ("d_sidx", af(SIDXO, 16), "sidxf"),
            ("d_q", ab(QBF, 2048), "qk:3"),
            ("d_k", ab(KBF, 2048), "qk:3"),
            ("d_vtm", ab(V_TM, 32768), "vev:15"),
            ("d_sqt", ab(SQTC, 8192), "sqev:3:15"),
        ]
        for name, src, gate in taps:
            mk.wait(sy, gate)
            mk.wait(sy, "yw:15")
            mk.inc(sy.dma_start(out=C[name].ap(), in_=src), "ds", 16, f"tap:{name}")
        for name, src in [("d_xnp", C["xnp_in"]), ("d_partial", C["partial"]),
                          ("d_eoin", C["eo_in"]), ("d_tab", C["tabl_red"]),
                          ("d_hist", C["hist_out"]), ("d_upark", C["upark"])]:
            mk.wait(gp, "yw:15")
            mk.inc(gp.dma_start(out=C[name].ap(), in_=src.ap()), "dg", 16, f"tap:{name}")

    for eng, sem in ((sy, "ds"), (gp, "dg"), (ac, "da")):
        if mk.record:
            mk.marks.setdefault(f"fin:{sem}", (sem, mk.cnt.get(sem, 0)))
        mk.wait(eng, f"fin:{sem}")


def build(debug=False):
    nc = bass.Bass("TRN2", target_bir_lowering=False)
    C = {}

    def inp(name, shape, dt=F32):
        h = nc.dram_tensor(name, shape, dt, kind="ExternalInput")
        return h.ap()

    C["x"] = inp("x", [TOK, D])
    C["wuv"] = inp("wuv", [D, 2 * E2 + SDIM])
    C["buv_row"] = inp("buv_row", [1, 2 * E2 + SDIM])
    C["wout"] = inp("wout", [E2, D])
    C["boutr"] = inp("boutr", [1, D])
    C["wg"] = inp("wg", [D, NEXP])
    C["w1"] = inp("w1", [D, FF])
    C["w2"] = inp("w2", [FF, D])
    C["b2r"] = inp("b2r", [1, D])
    in_sb = [
        ("ident", [128, 128], BF16), ("sut", [128, 128], F32),
        ("onesc", [128, 1], F32), ("ones1f", [1, 128], F32), ("ones1b", [1, 128], BF16),
        ("iota8", [128, 128], F32), ("psel", [8, 1], F32), ("tokv", [128, 16], F32),
        ("sgbase", [128, 1], U32), ("buvfm", [128, 33], F32), ("b1fm", [128, 32], F32),
        ("gq", [128, 1], F32), ("gk", [128, 1], F32), ("bq", [128, 1], F32),
        ("bk", [128, 1], F32),
    ]
    for name, shape, dt in in_sb:
        C[name + "_d"] = inp(name, shape, dt)
    C["zero_d"] = inp("zero", [128, 128], F32)

    C["y"] = nc.dram_tensor("y", [TOK, D], F32, kind="ExternalOutput")
    if debug:
        for name, shape, dt in [
            ("d_mu", [128, 16], F32), ("d_vv", [128, 16], F32),
            ("d_rstd", [128, 16], F32), ("d_sumx2", [128, 16], F32),
            ("d_xf", [128, 2048], F32),
            ("d_lg", [128, 128], F32), ("d_m1", [128, 128], F32),
            ("d_rank", [128, 128], F32), ("d_g0", [128, 16], F32),
            ("d_loc", [128, 16], F32), ("d_kept", [128, 16], F32),
            ("d_es", [128, 16], F32), ("d_gfin", [128, 16], F32),
            ("d_didx", [128, 16], F32), ("d_cidx", [128, 16], F32),
            ("d_inv", [128, 16], F32), ("d_sidx", [128, 16], F32),
            ("d_q", [128, 2048], BF16), ("d_k", [128, 2048], BF16),
            ("d_vtm", [128, 32768], BF16), ("d_sqt", [128, 8192], BF16),
            ("d_xnp", [TOK, D], BF16), ("d_partial", [TOK, D], F32),
            ("d_eoin", [TOK, D], BF16), ("d_tab", [TABN, 1], F32),
            ("d_hist", [8, 8], F32), ("d_upark", [NG, 128, 8192], BF16),
            ("d_slots", [128, 1024], BF16),
        ]:
            C[name] = nc.dram_tensor(name, shape, dt, kind="ExternalOutput")

    C["xnp_in"] = nc.dram_tensor("xnp_in", [TOK, D], BF16)
    C["xnp"] = nc.dram_tensor("xnp", [NCORE * TOK, D], BF16)
    C["hist_in"] = nc.dram_tensor("hist_in", [1, 8], F32)
    C["hist_out"] = nc.dram_tensor("hist_out", [8, 8], F32)
    C["tabl"] = nc.dram_tensor("tabl", [TABN, 1], F32)
    C["tabl_red"] = nc.dram_tensor("tabl_red", [TABN, 1], F32)
    C["upark"] = nc.dram_tensor("upark", [NG, 128, 8192], BF16)
    C["partial"] = nc.dram_tensor("partial", [TOK, D], F32)
    C["eo_in"] = nc.dram_tensor("eo_in", [TOK, D], BF16)
    C["eo_pool"] = nc.dram_tensor("eo_pool", [NCORE * TOK, D], BF16)
    C["wout_bf"] = nc.dram_tensor("wout_bf", [128, 16 * 1024], BF16)
    C["w1_bf"] = nc.dram_tensor("w1_bf", [128, 32 * 1024], BF16)
    C["fence_f_s"] = nc.dram_tensor("fence_f_s", [128, 128], F32)
    C["fence_f_a"] = nc.dram_tensor("fence_f_a", [128, 128], F32)
    C["fence_b_s"] = nc.dram_tensor("fence_b_s", [128, 1024], BF16)
    C["fence_b_a"] = nc.dram_tensor("fence_b_a", [128, 1024], BF16)
    C["fence_b"] = nc.dram_tensor("fence_b", [128, 1024], BF16)
    C["fence_f"] = nc.dram_tensor("fence_f", [128, 128], F32)

    with ExitStack() as ctx:
        e = ctx.enter_context
        C["AB"] = e(nc.sbuf_tensor("AB", [128, AB_COLS], BF16))[:]
        C["AFa"] = e(nc.sbuf_tensor("AFa", [128, AF_COLS], F32))[:]
        for name, shape, dt in in_sb:
            C[name + "_sb"] = e(nc.sbuf_tensor(name + "_sb", shape, dt))
        for name, shape, dt in [
            ("xnb0", [128, 1024], BF16),
            ("xnb1", [128, 1024], BF16), ("slots0", [128, 1024], BF16),
            ("slots1", [128, 1024], BF16), ("ffnb0", [128, 1024], BF16),
            ("ffnb1", [128, 1024], BF16), ("eob0", [128, 1024], BF16),
            ("eob1", [128, 1024], BF16), ("wgb_sb", [128, 64], BF16),
            ("buvv_sb", [1, 2048], BF16), ("boutr_sb", [1, 1024], BF16),
            ("b2r_sb", [1, 1024], BF16), ("csrow_sb", [1, 128], F32),
            ("rowa_sb", [1, 128], F32), ("rowb_sb", [1, 128], F32),
            ("hist_row_sb", [1, 8], F32),
            ("hist_sb", [8, 8], F32), ("offsr_sb", [1, 8], F32),
            ("didx_sb", [128, 16], U32), ("cidx_sb", [128, 16], U32),
            ("cidx2_sb", [128, 16], U32),
            ("sidx_sb", [128, 16], U32), ("loci_sb", [128, 16], I32),
            ("hi_sb", [128, 16], I32), ("lo_sb", [128, 16], I32),
            ("k9_sb", [128, 16], I32), ("l9_sb", [128, 16], I32),
        ]:
            C[name] = e(nc.sbuf_tensor(name, shape, dt))
        PS = {}
        for i in [0, 1, 2, 3]:
            PS[i] = e(nc.psum_tensor(f"PS{i}", [128, 512], F32))
        PT = [e(nc.psum_tensor("PT0", [128, 1024], BF16)),
              e(nc.psum_tensor("PT1", [128, 1024], BF16))]
        PS[6] = e(nc.psum_tensor("PS6", [128, 512], F32))
        PS[7] = e(nc.psum_tensor("PS7", [128, 512], F32))
        C["PS"] = PS
        C["PT"] = PT

        mk = Marks()
        sems = {}
        for s in ["ds", "dg", "da", "sv", "sa", "st",
                  "cc1", "cc2", "cc3", "c40", "c41", "c42", "c43"]:
            sems[s] = e(nc.semaphore(s))
        mk.sems = sems

        fake = _Fake()
        names = ["sy", "gp", "ac", "ve", "te"]
        E = {n: fake for n in names}
        _emit(E, mk, C, debug)
        mk.record = False
        block = e(nc.Block())

        def mk_section(name):
            def section(eng):
                E = {n: (eng if n == name else fake) for n in names}
                _emit(E, mk, C, debug)
            return section

        block.sync(mk_section("sy"))
        block.gpsimd(mk_section("gp"))
        block.scalar(mk_section("ac"))
        block.vector(mk_section("ve"))
        block.tensor(mk_section("te"))
    return nc


def _host_inputs(L0_attn_inputs, L0_attn_W_uv, L0_attn_b_uv, L0_attn_gamma,
                 L0_attn_beta, L0_attn_W_out, L0_attn_b_out, L0_ffn_wg,
                 L0_ffn_w1, L0_ffn_b1, L0_ffn_w2, L0_ffn_b2):
    f32 = np.float32
    x = np.ascontiguousarray(np.asarray(L0_attn_inputs, dtype=f32))
    wuv = np.ascontiguousarray(np.asarray(L0_attn_W_uv, dtype=f32))
    buv = np.asarray(L0_attn_b_uv, dtype=f32)
    gamma = np.asarray(L0_attn_gamma, dtype=f32)
    beta = np.asarray(L0_attn_beta, dtype=f32)
    wout = np.ascontiguousarray(np.asarray(L0_attn_W_out, dtype=f32))
    bout = np.asarray(L0_attn_b_out, dtype=f32)
    wg = np.ascontiguousarray(np.asarray(L0_ffn_wg, dtype=f32))
    w1 = np.ascontiguousarray(np.asarray(L0_ffn_w1, dtype=f32))
    b1 = np.asarray(L0_ffn_b1, dtype=f32)
    w2 = np.ascontiguousarray(np.asarray(L0_ffn_w2, dtype=f32))
    b2 = np.asarray(L0_ffn_b2, dtype=f32)

    ident = np.eye(128, dtype=ml_dtypes.bfloat16)
    sut = np.triu(np.ones((128, 128), f32), 1)
    onesc = np.ones((128, 1), f32)
    ones1f = np.ones((1, 128), f32)
    ones1b = np.ones((1, 128), dtype=ml_dtypes.bfloat16)
    iota8 = np.tile(np.arange(8, dtype=f32), (128, 16))
    buvfm = np.ascontiguousarray(buv.reshape(33, 128).T)
    b1fm_all = b1.reshape(NCORE, 32, 128)
    zero = np.zeros((128, 128), f32)

    maps = []
    for c in range(NCORE):
        tokv = (c * TOK + np.arange(16)[None, :] * 128
                + np.arange(128)[:, None] + 1).astype(f32)
        sgbase = (c * CAP + np.arange(128) * 16).astype(np.uint32)[:, None]
        psel = (np.arange(8) < c).astype(f32)[:, None]
        m = {
            "x": x[c], "wuv": wuv, "buv_row": buv[None, :], "wout": wout,
            "boutr": bout[None, :], "wg": wg, "w1": w1[c],
            "w2": w2[c], "b2r": b2[c][None, :],
            "ident": ident, "sut": sut, "onesc": onesc, "ones1f": ones1f,
            "ones1b": ones1b, "iota8": iota8, "psel": psel, "tokv": tokv,
            "sgbase": sgbase, "buvfm": buvfm,
            "b1fm": np.ascontiguousarray(b1fm_all[c].T),
            "gq": gamma[0][:, None], "gk": gamma[1][:, None],
            "bq": beta[0][:, None], "bk": beta[1][:, None],
            "zero": zero,
        }
        maps.append(m)
    return maps


def kernel(**inputs):
    maps = _host_inputs(**inputs)
    nc = build(debug=False)
    res = run_bass_kernel_spmd(nc, maps, core_ids=list(range(NCORE)))
    return np.stack([res.results[c]["y"] for c in range(NCORE)], axis=0)
